# version 01
# baseline (speedup 1.0000x reference)
import numpy as np
import ml_dtypes
import dataclasses
from contextlib import ExitStack

import concourse.bass as bass
import concourse.mybir as mybir
from concourse.bass_utils import run_bass_kernel_spmd
from concourse.bass import IndirectOffsetOnAxis
from concourse.bass_types import AP
from concourse.expressions import RuntimeValue

F32 = mybir.dt.float32
BF16 = mybir.dt.bfloat16
I32 = mybir.dt.int32
U32 = mybir.dt.uint32
ALU = mybir.AluOpType
AF = mybir.ActivationFunctionType
AX = mybir.AxisListType

T16 = 16
NTOK = 2048
D = 1024
DCH = 8
NH = 16
DH = 64
NE = 32
CAP = 1024
NI = 512
EPS = 1e-5
BIGIDX = float(1 << 20)

STAGES = ["ln", "a", "f1", "b", "full"]
_STAGE = "full"


def bc_mid(ap2d, n):
    return dataclasses.replace(ap2d, ap=[ap2d.ap[0], [0, n], ap2d.ap[1]])


class Prog:
    def __init__(self):
        self.nc = bass.Bass("TRN2", target_bir_lowering=False, debug=False)
        self.ctx = ExitStack()
        self.cnt = {}
        self.sem = {}
        for n in ["sT", "sV", "sS", "sG", "dY", "dS", "dG", "AG"]:
            self.sem[n] = self.ctx.enter_context(self.nc.semaphore(n))
            self.cnt[n] = 0

    def inc(self, inst, name, k=1):
        inst.then_inc(self.sem[name], k)
        self.cnt[name] += k
        return inst

    def dinc(self, inst, name):
        return self.inc(inst, name, 16)

    def wait(self, eng, *names):
        for n in names:
            eng.wait_ge(self.sem[n], self.cnt[n])

    def edge(self, eng, name):
        self.inc(eng.engine_nop() if hasattr(eng, "engine_nop") else eng.nop(), name)
        eng.wait_ge(self.sem[name], self.cnt[name])

    def sb(self, name, shape, dt):
        return self.ctx.enter_context(self.nc.sbuf_tensor(name, shape, dt))

    def psum(self, name, shape, dt=F32):
        return self.ctx.enter_context(self.nc.psum_tensor(name, shape, dt))


def build_program(stage="full", dbg=False):
    p = Prog()
    nc = p.nc
    inc, dinc, wait = p.inc, p.dinc, p.wait
    S_ = p.sem
    C_ = p.cnt

    def ein(name, shape, dt):
        return nc.dram_tensor(name, shape, dt, kind="ExternalInput").ap()

    xin = ein("xin", [NTOK, D], F32)
    yout = nc.dram_tensor("yout", [NTOK, D], F32, kind="ExternalOutput").ap()

    aWq = ein("aWq", [D, D], F32)
    aWkv = ein("aWkv", [D, 128], F32)
    aWout = ein("aWout", [D, D], F32)
    bwqT = ein("bwqT", [D, D], F32)
    bwkT = ein("bwkT", [D, D], F32)
    bwvT = ein("bwvT", [D, D], F32)
    bwoT = ein("bwoT", [D, D], F32)
    gateT = ein("gateT", [D, NE], BF16)
    W1 = ein("W1", [NE, D, NI], BF16)
    W3 = ein("W3", [NE, D, NI], BF16)
    W2 = ein("W2", [NE, NI, D], BF16)
    sw1T = ein("sw1T", [D, NI], BF16)
    sw3T = ein("sw3T", [D, NI], BF16)
    sw2T = ein("sw2T", [NI, D], BF16)
    gamma_bc = ein("gamma_bc", [128, D], F32)
    qscale_bc = ein("qscale_bc", [128, D], F32)
    kscale_bc = ein("kscale_bc", [128, DH], F32)
    qsrow = ein("qsrow", [1, DH], F32)
    ksrow = ein("ksrow", [1, DH], F32)
    nullk = ein("nullk", [2, DH], F32)
    nullv = ein("nullv", [2, DH], F32)
    biasg_bc = ein("biasg_bc", [128, NE], F32)
    prefmask = ein("prefmask", [8, 1], F32)
    identbf = ein("identbf", [128, 128], BF16)
    triUf = ein("triUf", [128, 128], F32)
    onesrow = ein("onesrow", [1, 128], F32)
    onescol = ein("onescol", [128, 1], F32)
    eidx_bc = ein("eidx_bc", [128, NE], F32)

    sp0 = nc.dram_tensor("sp0", [NTOK, D], BF16, kind="Internal").ap()
    sp1 = nc.dram_tensor("sp1", [NTOK, D], BF16, kind="Internal").ap()
    xe_dram = nc.dram_tensor("xe_dram", [NE * CAP, D], BF16, kind="Internal").ap()
    ye_dram = nc.dram_tensor("ye_dram", [NE * CAP, D], F32, kind="Internal").ap()
    ag_in = nc.dram_tensor("ag_in", [1, NE], F32, kind="Internal").ap()
    ag_out = nc.dram_tensor("ag_out", [8, NE], F32, kind="Internal").ap()

    res = p.sb("res", [128, T16 * D], F32)
    P0 = p.sb("P0", [128, T16 * D], BF16)
    P1 = p.sb("P1", [128, T16 * D], BF16)
    HB = p.sb("HB", [128, 6272], BF16)
    WB = p.sb("WB", [128, 12 * 1024], BF16)
    QC = p.sb("QC", [128, 4096], BF16)
    OF = p.sb("OF", [128, 2048], BF16)
    SC2 = p.sb("SC2", [128, 2048], F32)
    GB = p.sb("GB", [128, 2048], F32)

    st12 = p.sb("st12", [128, T16, 12], F32)
    mv2 = p.sb("mv2", [128, T16, 2], F32)
    lnS = p.sb("lnS", [128, 128], F32)
    sgate = p.sb("sgate", [128, T16 * NE], F32)
    mxs = p.sb("mxs", [128, T16 * 8], F32)
    mis = p.sb("mis", [128, T16 * 8], U32)
    mif = p.sb("mif", [128, T16 * 4], F32)
    oh4 = p.sb("oh4", [128, 4 * NE], F32)
    ohw = p.sb("ohw", [128, 4 * NE], F32)
    ohT = p.sb("ohT", [128, T16 * NE], BF16)
    w4 = p.sb("w4", [128, 4], F32)
    wsc = p.sb("wsc", [128, 8], F32)
    g4 = p.sb("g4", [128, T16 * 4], F32)
    gfin = p.sb("gfin", [128, T16 * 4], F32)
    idx4 = p.sb("idx4", [128, T16 * 4], I32)
    ctile = p.sb("ctile", [1, T16 * NE], F32)
    offrow = p.sb("offrow", [1, (T16 + 1) * NE], F32)
    cntrow = p.sb("cntrow", [1, NE], F32)
    agbuf = p.sb("agbuf", [8, NE], F32)
    keptf = p.sb("keptf", [1, NE], F32)
    kepti = p.sb("kepti", [1, NE], I32)
    scal = p.sb("scal", [128, 8], F32)
    c_onesrow = p.sb("c_onesrow", [1, 128], F32)
    c_onescol = p.sb("c_onescol", [128, 1], F32)
    c_onescol_bf = p.sb("c_onescol_bf", [128, 1], BF16)
    negb = p.sb("negb", [128, 2], F32)
    qhn = p.sb("qhn", [128, NH], F32)
    qhn2 = p.sb("qhn2", [128, NH], F32)
    kno = p.sb("kno", [128, 2], F32)
    knp = p.sb("knp", [128, DH], BF16)
    knullf = p.sb("knullf", [128, 2], BF16)
    vnaug = p.sb("vnaug", [2, 65], BF16)

    srow_s = SC2[0:1, 0:512]
    srow_r = SC2[0:1, 512:1024]
    kscr = SC2[:, 1024:1024 + DH]
    qsr = SC2[0:1, 1088:1088 + DH]
    ksr = SC2[0:1, 1152:1152 + DH]
    c_kscale = SC2[:, 1216:1216 + DH]
    c_gamma = SC2[:, 0:D]
    c_biasg = SC2[:, 0:NE]
    c_eidx = SC2[:, NE:2 * NE]
    c_prefmask = SC2[0:8, 2 * NE:2 * NE + 1]
    c_qscale = GB[:, 1024:1024 + D]
    slotm = GB[:, 0:512]
    c_ident = HB[:, 4112:4112 + 128]
    c_triU = HB[:, 4240:4240 + 128]
    rbc = SC2[0:64, 1024:1536]
    kFa = HB[:, 1024:1024 + NTOK]
    vauga = HB[:, 3072 + 0:3072 + T16 * 65]

    PS = [p.psum(f"ps{i}", [128, 512]) for i in range(7)]
    PST = p.psum("pst", [128, 128], BF16)

    rv = res[:].rearrange("p (t d) -> p t d", d=D)
    p0v = P0[:].rearrange("p (t d) -> p t d", d=D)
    p1v = P1[:].rearrange("p (c t) -> p c t", t=NTOK)
    qcv = QC[:].rearrange("p (r d) -> p r d", d=D)
    qcg = QC[:].rearrange("p (c t) -> p c t", t=512)
    gbv = GB[:].rearrange("p (r d) -> p r d", d=D)
    ofv = OF[:].rearrange("p (r t) -> p r t", t=512)
    vav = vauga.rearrange("p (t a) -> p t a", a=65)
    hbv = HB[:, 0:4096].rearrange("p (r t) -> p r t", t=2048)

    with nc.Block() as block:
        onY, onT, onV, onS, onG = block.sync, block.tensor, block.vector, block.scalar, block.gpsimd

        def _(y):
            dinc(y.dma_start(out=rv[:], in_=xin.rearrange("(t p) d -> p t d", p=128)), "dY")
            dinc(y.dma_start(out=qsr, in_=qsrow), "dY")
            dinc(y.dma_start(out=ksr, in_=ksrow), "dY")
            dinc(y.dma_start(out=c_onesrow[:], in_=onesrow), "dY")
            dinc(y.dma_start(out=c_onescol[:], in_=onescol), "dY")
        onY(_)

        bndreg = [None]

        def _(g):
            bndreg[0] = g.alloc_register("bnd")
            g.reg_mov(bndreg[0], NE * CAP - 1)
            inc(g.memset(GB[:], 0.0), "sG")
            g.wait_ge(S_["sG"], C_["sG"])
            zv = xe_dram.rearrange("(n p) d -> p n d", p=128)
            gz = GB[:, 0:2048].rearrange("p (n d) -> p n d", d=D)
            for i in range(NE * CAP // 256):
                dinc(g.dma_start(out=zv[:, 2 * i:2 * i + 2, :], in_=gz), "dG")
        onG(_)

        def ln_phase(with_xn):
            def _(v):
                wait(v, "dY", "sS", "sT", "sG", "dS", "dG")
                v.wait_ge(S_["sV"], C_["sV"])
                for ti in range(T16):
                    inc(v.bn_stats(st12[:, ti, 0:6], rv[:, ti, 0:512]), "sV")
                    inc(v.bn_stats(st12[:, ti, 6:12], rv[:, ti, 512:1024]), "sV")
                v.wait_ge(S_["sV"], C_["sV"])
                for ti in range(T16):
                    inc(v.bn_aggr(mv2[:, ti, :], st12[:, ti, 0:12]), "sV")
                v.wait_ge(S_["sV"], C_["sV"])
                inc(v.tensor_scalar_add(lnS[:, 0:16], mv2[:, :, 1], EPS), "sV")
                v.wait_ge(S_["sV"], C_["sV"])
                inc(v.reciprocal(lnS[:, 16:32], lnS[:, 0:16]), "sV")
            onV(_)

            def _(s):
                wait(s, "sV")
                inc(s.activation(out=lnS[:, 32:48], in_=lnS[:, 16:32], func=AF.Sqrt), "sS")
            onS(_)

            def _(v):
                wait(v, "sS")
                inc(v.scalar_tensor_tensor(out=lnS[:, 48:64], in0=mv2[:, :, 0], scalar=-1.0,
                                           in1=lnS[:, 32:48], op0=ALU.mult, op1=ALU.mult), "sV")
            onV(_)

            def _(s):
                wait(s, "sV")
                for ti in range(T16):
                    inc(s.activation(out=p0v[:, ti, :], in_=rv[:, ti, :], func=AF.Identity,
                                     bias=lnS[:, 48 + ti:49 + ti], scale=lnS[:, 32 + ti:33 + ti]), "sS")
            onS(_)

            def _(y):
                wait(y, "sS")
                dinc(y.dma_start(out=sp0.rearrange("(t p) d -> p t d", p=128), in_=p0v[:]), "dY")
            onY(_)

            def _(s):
                wait(s, "dY")
                for c in range(DCH):
                    dinc(s.dma_start(out=p1v[:, c, :], in_=sp0[:, 128 * c:128 * (c + 1)],
                                     transpose=True), "dS")
            onS(_)

            if with_xn:
                def _(v):
                    wait(v, "sS")
                    inc(v.tensor_mul(lnS[:, 64:80], lnS[:, 32:48], lnS[:, 32:48]), "sV")
                    v.wait_ge(S_["sV"], C_["sV"])
                    inc(v.tensor_mul(lnS[:, 64:80], lnS[:, 64:80], mv2[:, :, 1]), "sV")
                    v.wait_ge(S_["sV"], C_["sV"])
                    inc(v.tensor_scalar_add(lnS[:, 64:80], lnS[:, 64:80], EPS), "sV")
                    v.wait_ge(S_["sV"], C_["sV"])
                    inc(v.reciprocal(lnS[:, 80:96], lnS[:, 64:80]), "sV")
                onV(_)

                def _(s):
                    wait(s, "sV")
                    inc(s.activation(out=lnS[:, 96:112], in_=lnS[:, 80:96], func=AF.Sqrt), "sS")
                onS(_)

                def _(v):
                    wait(v, "sS")
                    inc(v.tensor_mul(lnS[:, 96:112], lnS[:, 96:112], lnS[:, 32:48]), "sV")
                    v.wait_ge(S_["sV"], C_["sV"])
                    inc(v.scalar_tensor_tensor(out=lnS[:, 112:128], in0=mv2[:, :, 0], scalar=-1.0,
                                               in1=lnS[:, 96:112], op0=ALU.mult, op1=ALU.mult), "sV")
                onV(_)

                def _(y):
                    dinc(y.dma_start(out=c_gamma, in_=gamma_bc), "dY")
                onY(_)

                xnv = sp1.rearrange("(t p) d -> p t d", p=128)
                for grp in range(8):
                    def _(s, grp=grp):
                        wait(s, "sV")
                        for j in range(2):
                            ti = grp * 2 + j
                            inc(s.activation(out=gbv[:, j, :], in_=rv[:, ti, :], func=AF.Identity,
                                             bias=lnS[:, 112 + ti:113 + ti],
                                             scale=lnS[:, 96 + ti:97 + ti]), "sS")
                    onS(_)

                    def _(v, grp=grp):
                        wait(v, "sS", "dY")
                        for j in range(2):
                            inc(v.tensor_mul(qcv[:, j, :], gbv[:, j, :], c_gamma), "sV")
                    onV(_)

                    def _(y, grp=grp):
                        wait(y, "sV")
                        dinc(y.dma_start(out=xnv[:, 2 * grp:2 * grp + 2, :],
                                         in_=qcv[:, 0:2, :]), "dY")
                    onY(_)

        def attn_a():
            ln_phase(with_xn=True)
            wkv_v = HB[:, 0:8 * 128].rearrange("p (a e) -> p a e", e=128)
            wq_v = WB[:, 0:8 * 1024].rearrange("p (a e) -> p a e", e=D)
            wout_v = WB[:, 0:8 * 1024].rearrange("p (a e) -> p a e", e=D)

            def _(g):
                dinc(g.dma_start(out=c_ident, in_=identbf), "dG")
                dinc(g.dma_start(out=c_kscale, in_=kscale_bc), "dG")
                dinc(g.dma_start(out=wkv_v[:], in_=aWkv.rearrange("(a p) e -> p a e", p=128)), "dG")
                dinc(g.dma_start(out=wq_v[:], in_=aWq.rearrange("(a p) e -> p a e", p=128)), "dG")
                inc(g.memset(knp[:], 0.0), "sG")
                inc(g.memset(vnaug[:], 1.0), "sG")
                inc(g.memset(vauga, 1.0), "sG")
                g.wait_ge(S_["sG"], C_["sG"])
                dinc(g.dma_start(out=vnaug[0:2, 0:64], in_=nullv), "dG")
                dinc(g.dma_start(out=kscr[0:2, :], in_=nullk), "dG")
            onG(_)

            def _(v):
                wait(v, "dY")
                inc(v.tensor_reduce(out=scal[0:1, 4:5], in_=qsr, axis=AX.X, op=ALU.max,
                                    apply_absolute_value=True), "sV")
                inc(v.tensor_reduce(out=scal[0:1, 5:6], in_=ksr, axis=AX.X, op=ALU.max,
                                    apply_absolute_value=True), "sV")
                v.wait_ge(S_["sV"], C_["sV"])
                inc(v.tensor_mul(scal[0:1, 6:7], scal[0:1, 4:5], scal[0:1, 5:6]), "sV")
                v.wait_ge(S_["sV"], C_["sV"])
                inc(v.tensor_scalar_mul(scal[0:1, 6:7], scal[0:1, 6:7], -8.0), "sV")
            onV(_)

            def _(t):
                wait(t, "sV")
                inc(t.matmul(PS[3][:, 0:1], c_onesrow[:], scal[0:1, 6:7], start=True, stop=True), "sT")
            onT(_)

            def _(v):
                wait(v, "sT")
                inc(v.tensor_copy(negb[:, 0:1], PS[3][:, 0:1]), "sV")
            onV(_)

            def _(s):
                wait(s, "dG")
                inc(s.activation(out=SC2[0:2, 0:64], in_=kscr[0:2, :], func=AF.Square,
                                 accum_out=kno[0:2, 0:1]), "sS")
                s.wait_ge(S_["sS"], C_["sS"])
                inc(s.activation(out=kno[0:2, 0:1], in_=kno[0:2, 0:1], func=AF.Sqrt), "sS")
            onS(_)

            def _(v):
                wait(v, "sS")
                inc(v.tensor_scalar_max(kno[0:2, 0:1], kno[0:2, 0:1], 1e-12), "sV")
                v.wait_ge(S_["sV"], C_["sV"])
                inc(v.reciprocal(kno[0:2, 1:2], kno[0:2, 0:1]), "sV")
                v.wait_ge(S_["sV"], C_["sV"])
                inc(v.tensor_scalar(out=kscr[0:2, :], in0=kscr[0:2, :], scalar1=kno[0:2, 1:2],
                                    scalar2=8.0, op0=ALU.mult, op1=ALU.mult), "sV")
                v.wait_ge(S_["sV"], C_["sV"])
                inc(v.tensor_mul(knp[0:2, :], kscr[0:2, :], c_kscale[0:2, :]), "sV")
            onV(_)

            def _(t):
                wait(t, "sV", "dY", "dG")
                t.transpose(PST[0:64, 0:128], knp[:, 0:64], c_ident)
                inc(t.transpose(PST[64:128, 0:128], knp[:, 0:64], c_ident), "sT")
            onT(_)

            def _(v):
                wait(v, "sT")
                inc(v.tensor_copy(knullf[:, 0:2], PST[:, 0:2]), "sV")
                v.wait_ge(S_["sV"], C_["sV"])
                inc(v.memset(knp[:], 0.0), "sV")
            onV(_)

            for ti in range(T16):
                def _(t, ti=ti):
                    if ti == 0:
                        wait(t, "dS", "dG", "sG", "sV")
                    if ti >= 2:
                        wait(t, "sV")
                    bank = PS[ti % 2]
                    for a in range(DCH):
                        mm = t.matmul(bank[:, 0:128], p1v[:, a, 128 * ti:128 * (ti + 1)],
                                      wkv_v[:, a, :], start=(a == 0), stop=(a == DCH - 1))
                    inc(mm, "sT")
                onT(_)

                def _(s, ti=ti):
                    wait(s, "sT")
                    bank = PS[ti % 2]
                    inc(s.activation(out=kscr[:, 0:DH], in_=bank[:, 0:DH], func=AF.Square,
                                     accum_out=kno[:, 0:1]), "sS")
                    s.wait_ge(S_["sS"], C_["sS"])
                    inc(s.activation(out=kno[:, 0:1], in_=kno[:, 0:1], func=AF.Sqrt), "sS")
                onS(_)

                def _(v, ti=ti):
                    wait(v, "sS")
                    bank = PS[ti % 2]
                    inc(v.tensor_copy(vav[:, ti, 0:64], bank[:, 64:128]), "sV")
                    inc(v.tensor_scalar_max(kno[:, 0:1], kno[:, 0:1], 1e-12), "sV")
                    v.wait_ge(S_["sV"], C_["sV"])
                    inc(v.reciprocal(kno[:, 1:2], kno[:, 0:1]), "sV")
                    v.wait_ge(S_["sV"], C_["sV"])
                    inc(v.tensor_scalar(out=kscr[:, 0:DH], in0=bank[:, 0:DH], scalar1=kno[:, 1:2],
                                        scalar2=8.0, op0=ALU.mult, op1=ALU.mult), "sV")
                    v.wait_ge(S_["sV"], C_["sV"])
                    inc(v.tensor_mul(knp[:, 0:DH], kscr[:, 0:DH], c_kscale), "sV")
                onV(_)

                def _(t, ti=ti):
                    wait(t, "sV")
                    t.transpose(PST[0:64, 0:128], knp[:, 0:64], c_ident)
                    inc(t.transpose(PST[64:128, 0:128], knp[:, 0:64], c_ident), "sT")
                onT(_)

                def _(v, ti=ti):
                    wait(v, "sT")
                    inc(v.tensor_copy(kFa[:, 128 * ti:128 * (ti + 1)], PST[:, 0:128]), "sV")
                onV(_)

            def _(s):
                wait(s, "sV", "dY")
                for c in range(DCH):
                    dinc(s.dma_start(out=p1v[:, c, :], in_=sp1[:, 128 * c:128 * (c + 1)],
                                     transpose=True), "dS")
            onS(_)

            def _(g):
                wait(g, "sV")
                dinc(g.dma_start(out=c_qscale, in_=qscale_bc), "dG")
            onG(_)

            qdv = sp0.rearrange("(t p) d -> p t d", p=128)
            for ti in range(T16):
                def _(t, ti=ti):
                    if ti == 0:
                        wait(t, "dS", "dG")
                    if ti >= 1:
                        wait(t, "sV")
                    for h2 in range(2):
                        for a in range(DCH):
                            mm = t.matmul(PS[h2][:, 0:512], p1v[:, a, 128 * ti:128 * (ti + 1)],
                                          wq_v[:, a, 512 * h2:512 * (h2 + 1)],
                                          start=(a == 0), stop=(a == DCH - 1))
                    inc(mm, "sT")
                onT(_)

                def _(v, ti=ti):
                    wait(v, "sT")
                    inc(v.tensor_copy(SC2[:, 0:512], PS[0][:, 0:512]), "sV")
                    inc(v.tensor_copy(SC2[:, 512:1024], PS[1][:, 0:512]), "sV")
                onV(_)

                def _(s, ti=ti):
                    wait(s, "sV")
                    inc(s.activation(out=GB[:, 0:1024], in_=SC2[:, 0:1024], func=AF.Square), "sS")
                onS(_)

                def _(v, ti=ti):
                    wait(v, "sS")
                    inc(v.tensor_reduce(out=qhn[:, 0:16], in_=GB[:, 0:1024].rearrange(
                        "p (h d) -> p h d", d=DH), axis=AX.X, op=ALU.add), "sV")
                onV(_)

                def _(s, ti=ti):
                    wait(s, "sV")
                    inc(s.activation(out=qhn2[:, 0:16], in_=qhn[:, 0:16], func=AF.Sqrt), "sS")
                onS(_)

                def _(v, ti=ti):
                    wait(v, "sS")
                    inc(v.tensor_scalar_max(qhn2[:, 0:16], qhn2[:, 0:16], 1e-12), "sV")
                    v.wait_ge(S_["sV"], C_["sV"])
                    inc(v.reciprocal(qhn[:, 0:16], qhn2[:, 0:16]), "sV")
                    v.wait_ge(S_["sV"], C_["sV"])
                    inc(v.tensor_tensor(out=SC2[:, 0:1024].rearrange("p (h d) -> p h d", d=DH),
                                        in0=SC2[:, 0:1024].rearrange("p (h d) -> p h d", d=DH),
                                        in1=qhn[:, 0:16].to_broadcast([128, 16, DH]),
                                        op=ALU.mult), "sV")
                    v.wait_ge(S_["sV"], C_["sV"])
                    inc(v.tensor_mul(qcv[:, ti % 4, :], SC2[:, 0:1024], c_qscale), "sV")
                onV(_)

                def _(y, ti=ti):
                    wait(y, "sV")
                    dinc(y.dma_start(out=qdv[:, ti, :], in_=qcv[:, ti % 4, :]), "dY")
                onY(_)

            def _(g):
                wait(g, "sT")
                dinc(g.dma_start(out=wout_v[:], in_=aWout.rearrange("(a p) e -> p a e", p=128)), "dG")
            onG(_)

            for g in range(4):
                def _(s, g=g):
                    wait(s, "dY")
                    wait(s, "sT")
                    for c in range(DCH):
                        dinc(s.dma_start(out=qcg[:, c, :],
                                         in_=sp0[512 * g:512 * (g + 1), 128 * c:128 * (c + 1)],
                                         transpose=True), "dS")
                onS(_)

                for hg in range(4):
                    pvb = [PS[3], PS[4], PS[5], PS[6]]
                    nkj = 4 * g + 4
                    for kj in range(nkj):
                        for hh in range(4):
                            h = 4 * hg + hh
                            seq = kj * 4 + hh
                            sb_ = PS[seq % 2]
                            slot = seq % 4

                            def _(t, g=g, kj=kj, hh=hh, h=h, sb_=sb_, slot=slot):
                                if kj == 0 and hh == 0:
                                    wait(t, "dS", "sV")
                                wait(t, "sS")
                                hb_ = (h % 2) * 64
                                mm = t.matmul(sb_[:, 0:512],
                                              kFa[hb_:hb_ + 64, 128 * kj:128 * (kj + 1)],
                                              qcg[:, h // 2, :][hb_:hb_ + 64, :],
                                              start=True, stop=True)
                                inc(mm, "sT")
                            onT(_)

                            diag = kj >= 4 * g

                            def _(s, sb_=sb_, slot=slot):
                                wait(s, "sT", "sG")
                                inc(s.activation(out=ofv[:, slot, :], in_=sb_[:, 0:512],
                                                 func=AF.Exp, bias=negb[:, 0:1], scale=1.0), "sS")
                            onS(_)

                            if diag:
                                o = kj - 4 * g

                                def _(gp, slot=slot, o=o, g=g, kj=kj):
                                    wait(gp, "sS")
                                    if g == 0 and kj == 0:
                                        gp.memset(ofv[:, slot, 0:64][64:128, :], 0.0)
                                        inc(gp.affine_select(
                                            out=ofv[:, slot, 64:512], in_=ofv[:, slot, 64:512],
                                            pattern=[[1, 448]], compare_op=ALU.is_ge,
                                            fill=0.0, base=64, channel_multiplier=-1), "sG")
                                    else:
                                        inc(gp.affine_select(
                                            out=ofv[:, slot, :], in_=ofv[:, slot, :],
                                            pattern=[[1, 512]], compare_op=ALU.is_ge,
                                            fill=0.0, base=-128 * o, channel_multiplier=-1), "sG")
                                onG(_)

                            def _(t, kj=kj, hh=hh, slot=slot, diag=diag, nkj=nkj):
                                wait(t, "sS")
                                if diag:
                                    wait(t, "sG")
                                inc(t.matmul(pvb[hh][0:65, 0:512], vav[:, kj, :], ofv[:, slot, :],
                                             start=(kj == 0), stop=False), "sT")
                            onT(_)

                    for hh in range(4):
                        h = 4 * hg + hh
                        slot = hh % 4

                        def _(t, h=h):
                            wait(t, "sS")
                            hb_ = (h % 2) * 64
                            inc(t.matmul(PS[2][0:2, 0:512], knullf[hb_:hb_ + 64, 0:2],
                                         qcg[:, h // 2, :][hb_:hb_ + 64, :],
                                         start=True, stop=True), "sT")
                        onT(_)

                        def _(s, slot=slot):
                            wait(s, "sT", "sG")
                            inc(s.activation(out=ofv[:, slot, :][0:2, :], in_=PS[2][0:2, 0:512],
                                             func=AF.Exp, bias=negb[0:2, 0:1], scale=1.0), "sS")
                        onS(_)

                        def _(t, hh=hh, slot=slot):
                            wait(t, "sS")
                            inc(t.matmul(pvb[hh][0:65, 0:512], vnaug[0:2, :], ofv[:, slot, :][0:2, :],
                                         start=False, stop=True), "sT")
                        onT(_)

                    for hh in range(4):
                        h = 4 * hg + hh

                        def _(v, hh=hh):
                            wait(v, "sT")
                            inc(v.tensor_copy(srow_s, pvb[hh][64:65, 0:512]), "sV")
                            v.wait_ge(S_["sV"], C_["sV"])
                            inc(v.reciprocal(srow_r, srow_s), "sV")
                        onV(_)

                        def _(t, hh=hh):
                            wait(t, "sV")
                            inc(t.matmul(PS[0][0:64, 0:512], c_onesrow[0:1, 0:64], srow_r,
                                         start=True, stop=True), "sT")
                        onT(_)

                        def _(v, hh=hh, h=h, g=g):
                            wait(v, "sT")
                            inc(v.tensor_copy(rbc, PS[0][0:64, 0:512]), "sV")
                            v.wait_ge(S_["sV"], C_["sV"])
                            dst = p1v[:, h // 2, 512 * g:512 * (g + 1)][(h % 2) * 64:(h % 2) * 64 + 64, :]
                            inc(v.tensor_mul(dst, pvb[hh][0:64, 0:512], rbc), "sV")
                        onV(_)

            for ti in range(T16):
                for dd in range(2):
                    def _(t, ti=ti, dd=dd):
                        if ti == 0 and dd == 0:
                            wait(t, "sV", "dG")
                        if ti >= 1 and dd == 0:
                            wait(t, "sV")
                        for c in range(DCH):
                            mm = t.matmul(PS[dd][:, 0:512],
                                          p1v[:, c, 128 * ti:128 * (ti + 1)],
                                          wout_v[:, c, 512 * dd:512 * (dd + 1)],
                                          start=(c == 0), stop=(c == DCH - 1))
                        inc(mm, "sT")
                    onT(_)

                def _(v, ti=ti):
                    wait(v, "sT")
                    inc(v.tensor_add(rv[:, ti, 0:512], rv[:, ti, 0:512], PS[0][:, 0:512]), "sV")
                    inc(v.tensor_add(rv[:, ti, 512:1024], rv[:, ti, 512:1024], PS[1][:, 0:512]), "sV")
                onV(_)

        def moe_ffn(callid):
            ln_phase(with_xn=False)
            gate_v = QC[:, 0:8 * NE].rearrange("p (a e) -> p a e", e=NE)

            def _(g):
                dinc(g.dma_start(out=gate_v[:], in_=gateT.rearrange("(a p) e -> p a e", p=128)), "dG")
                dinc(g.dma_start(out=c_triU, in_=triUf), "dG")
                dinc(g.dma_start(out=c_onescol_bf[:], in_=onescol), "dG")
            onG(_)

            def _(y):
                wait(y, "sV", "sS")
                dinc(y.dma_start(out=c_biasg, in_=biasg_bc), "dY")
                dinc(y.dma_start(out=c_eidx, in_=eidx_bc), "dY")
                dinc(y.dma_start(out=c_prefmask, in_=prefmask), "dY")
            onY(_)

            def _(t):
                wait(t, "dS", "dG", "sS", "sV")
                for ti in range(T16):
                    for a in range(DCH):
                        mm = t.matmul(PS[2][:, 32 * ti:32 * (ti + 1)],
                                      p1v[:, a, 128 * ti:128 * (ti + 1)], gate_v[:, a, :],
                                      start=(a == 0), stop=(a == DCH - 1))
                inc(mm, "sT")
            onT(_)

            def _(s):
                wait(s, "sT")
                inc(s.activation(out=sgate[:], in_=PS[2][:, 0:512], func=AF.Sigmoid), "sS")
            onS(_)

            def _(v):
                wait(v, "sS", "dY")
                inc(v.tensor_tensor(out=slotm.rearrange("p (t e) -> p t e", e=NE),
                                    in0=sgate[:].rearrange("p (t e) -> p t e", e=NE),
                                    in1=bc_mid(c_biasg, T16), op=ALU.add), "sV")
                v.wait_ge(S_["sV"], C_["sV"])
                for ti in range(T16):
                    inc(v.max(out=mxs[:, 8 * ti:8 * ti + 8], in_=slotm[:, 32 * ti:32 * (ti + 1)]), "sV")
                v.wait_ge(S_["sV"], C_["sV"])
                for ti in range(T16):
                    inc(v.max_index(out=mis[:, 8 * ti:8 * ti + 8], in_max=mxs[:, 8 * ti:8 * ti + 8],
                                    in_values=slotm[:, 32 * ti:32 * (ti + 1)]), "sV")
                v.wait_ge(S_["sV"], C_["sV"])
                inc(v.tensor_copy(mif[:].rearrange("p (t k) -> p t k", k=4),
                                  mis[:].rearrange("p (t k) -> p t k", k=8)[:, :, 0:4]), "sV")
            onV(_)

            for ti in range(T16):
                def _(v, ti=ti):
                    wait(v, "sV")
                    for k in range(4):
                        inc(v.tensor_scalar(out=oh4[:, 32 * k:32 * (k + 1)], in0=c_eidx,
                                            scalar1=mif[:, 4 * ti + k:4 * ti + k + 1], scalar2=None,
                                            op0=ALU.is_equal), "sV")
                    v.wait_ge(S_["sV"], C_["sV"])
                    inc(v.tensor_add(ohw[:, 0:32], oh4[:, 0:32], oh4[:, 32:64]), "sV")
                    inc(v.tensor_add(ohw[:, 32:64], oh4[:, 64:96], oh4[:, 96:128]), "sV")
                    v.wait_ge(S_["sV"], C_["sV"])
                    inc(v.tensor_add(ohT[:, 32 * ti:32 * (ti + 1)], ohw[:, 0:32], ohw[:, 32:64]), "sV")
                    v.wait_ge(S_["sV"], C_["sV"])
                    inc(v.tensor_tensor(out=ohw[:].rearrange("p (k e) -> p k e", e=NE),
                                        in0=oh4[:].rearrange("p (k e) -> p k e", e=NE),
                                        in1=bc_mid(sgate[:, 32 * ti:32 * (ti + 1)], 4),
                                        op=ALU.mult), "sV")
                    v.wait_ge(S_["sV"], C_["sV"])
                    inc(v.tensor_reduce(out=w4[:, 0:4], in_=ohw[:].rearrange("p (k e) -> p k e", e=NE),
                                        axis=AX.X, op=ALU.add), "sV")
                    v.wait_ge(S_["sV"], C_["sV"])
                    inc(v.tensor_reduce(out=wsc[:, 0:1], in_=w4[:, 0:4], axis=AX.X, op=ALU.add), "sV")
                    v.wait_ge(S_["sV"], C_["sV"])
                    inc(v.tensor_scalar_max(wsc[:, 1:2], wsc[:, 0:1], 1e-12), "sV")
                    v.wait_ge(S_["sV"], C_["sV"])
                    inc(v.reciprocal(wsc[:, 2:3], wsc[:, 1:2]), "sV")
                    v.wait_ge(S_["sV"], C_["sV"])
                    inc(v.tensor_scalar(out=g4[:, 4 * ti:4 * ti + 4], in0=w4[:, 0:4],
                                        scalar1=wsc[:, 2:3], scalar2=None, op0=ALU.mult), "sV")
                onV(_)

            for ti in range(T16):
                def _(t, ti=ti):
                    if ti == 0:
                        wait(t, "sV", "sS")
                    mm1 = t.matmul(PS[3][0:1, 32 * ti:32 * (ti + 1)], c_onescol_bf[:],
                                   ohT[:, 32 * ti:32 * (ti + 1)], start=True, stop=True)
                    mm2 = t.matmul(PS[4][0:1, 0:32], c_onescol_bf[:], ohT[:, 32 * ti:32 * (ti + 1)],
                                   start=(ti == 0), stop=(ti == T16 - 1))
                    if ti == T16 - 1:
                        inc(mm1, "sT")
                        inc(mm2, "sT")
                onT(_)

            def _(v):
                wait(v, "sT")
                inc(v.tensor_copy(ctile[:], PS[3][0:1, 0:512]), "sV")
                inc(v.tensor_copy(cntrow[:], PS[4][0:1, 0:32]), "sV")
            onV(_)

            def _(y):
                wait(y, "sV")
                dinc(y.dma_start(out=ag_in, in_=cntrow[:]), "dY")
            onY(_)

            def _(g):
                wait(g, "dY")
                inc(g.collective_compute("AllGather", ALU.bypass,
                                         replica_groups=[[0, 1, 2, 3, 4, 5, 6, 7]],
                                         ins=[ag_in], outs=[ag_out]), "AG")
            onG(_)

            def _(y):
                wait(y, "AG")
                dinc(y.dma_start(out=agbuf[:], in_=ag_out), "dY")
            onY(_)

            def _(t):
                wait(t, "dY", "sV")
                inc(t.matmul(PS[4][0:1, 0:32], c_prefmask, agbuf[:], start=True, stop=True), "sT")
            onT(_)

            ocrow = SC2[0:1, 96:128]
            obc = SC2[:, 96:128]

            def _(v):
                wait(v, "sT")
                inc(v.tensor_copy(ocrow, PS[4][0:1, 0:32]), "sV")
                inc(v.memset(offrow[0:1, 0:32], 0.0), "sV")
                v.wait_ge(S_["sV"], C_["sV"])
                for ti in range(T16):
                    inc(v.tensor_add(offrow[0:1, 32 * (ti + 1):32 * (ti + 2)],
                                     offrow[0:1, 32 * ti:32 * (ti + 1)],
                                     ctile[0:1, 32 * ti:32 * (ti + 1)]), "sV")
                    v.wait_ge(S_["sV"], C_["sV"])
                inc(v.tensor_scalar(out=keptf[:], in0=ocrow, scalar1=-1.0,
                                    scalar2=float(CAP), op0=ALU.mult, op1=ALU.add), "sV")
                v.wait_ge(S_["sV"], C_["sV"])
                inc(v.tensor_scalar_max(keptf[:], keptf[:], 0.0), "sV")
                v.wait_ge(S_["sV"], C_["sV"])
                inc(v.tensor_tensor(out=keptf[:], in0=keptf[:], in1=cntrow[:], op=ALU.min), "sV")
                v.wait_ge(S_["sV"], C_["sV"])
                inc(v.tensor_copy(kepti[:], keptf[:]), "sV")
            onV(_)

            def _(t):
                wait(t, "sV")
                inc(t.matmul(PS[5][:, 0:32], c_onesrow[:], ocrow, start=True, stop=True), "sT")
            onT(_)

            def _(v):
                wait(v, "sT")
                inc(v.tensor_copy(obc, PS[5][:, 0:32]), "sV")
            onV(_)

            for ti in range(T16):
                def _(t, ti=ti):
                    if ti == 0:
                        wait(t, "sV")
                    mm1 = t.matmul(PS[2][:, 32 * ti:32 * (ti + 1)], c_triU,
                                   ohT[:, 32 * ti:32 * (ti + 1)], start=True, stop=False)
                    mm2 = t.matmul(PS[2][:, 32 * ti:32 * (ti + 1)], c_onesrow[:],
                                   offrow[0:1, 32 * ti:32 * (ti + 1)], start=False, stop=True)
                    if ti == T16 - 1:
                        inc(mm2, "sT")
                onT(_)

            def _(v):
                wait(v, "sT")
                inc(v.scalar_tensor_tensor(out=slotm.rearrange("p (t e) -> p t e", e=NE),
                                           in0=bc_mid(c_eidx, T16), scalar=float(CAP),
                                           in1=PS[2][:, 0:512].rearrange("p (t e) -> p t e", e=NE),
                                           op0=ALU.mult, op1=ALU.add), "sV")
            onV(_)

            for ti in range(T16):
                def _(v, ti=ti):
                    wait(v, "sV")
                    for k in range(4):
                        inc(v.tensor_scalar(out=oh4[:, 32 * k:32 * (k + 1)], in0=c_eidx,
                                            scalar1=mif[:, 4 * ti + k:4 * ti + k + 1], scalar2=None,
                                            op0=ALU.is_equal), "sV")
                    v.wait_ge(S_["sV"], C_["sV"])
                    inc(v.tensor_tensor(out=ohw[:].rearrange("p (k e) -> p k e", e=NE),
                                        in0=oh4[:].rearrange("p (k e) -> p k e", e=NE),
                                        in1=bc_mid(slotm[:, 32 * ti:32 * (ti + 1)], 4),
                                        op=ALU.mult), "sV")
                    v.wait_ge(S_["sV"], C_["sV"])
                    inc(v.tensor_reduce(out=w4[:, 0:4], in_=ohw[:].rearrange("p (k e) -> p k e", e=NE),
                                        axis=AX.X, op=ALU.add), "sV")
                    v.wait_ge(S_["sV"], C_["sV"])
                    inc(v.tensor_tensor(out=ohw[:].rearrange("p (k e) -> p k e", e=NE),
                                        in0=oh4[:].rearrange("p (k e) -> p k e", e=NE),
                                        in1=bc_mid(obc, 4), op=ALU.mult), "sV")
                    v.wait_ge(S_["sV"], C_["sV"])
                    inc(v.tensor_reduce(out=wsc[:, 0:4], in_=ohw[:].rearrange("p (k e) -> p k e", e=NE),
                                        axis=AX.X, op=ALU.add), "sV")
                    v.wait_ge(S_["sV"], C_["sV"])
                    inc(v.scalar_tensor_tensor(out=wsc[:, 0:4], in0=mif[:, 4 * ti:4 * ti + 4],
                                               scalar=-float(CAP), in1=wsc[:, 0:4],
                                               op0=ALU.mult, op1=ALU.add), "sV")
                    v.wait_ge(S_["sV"], C_["sV"])
                    inc(v.tensor_add(wsc[:, 0:4], wsc[:, 0:4], w4[:, 0:4]), "sV")
                    v.wait_ge(S_["sV"], C_["sV"])
                    inc(v.tensor_scalar(out=wsc[:, 4:8], in0=wsc[:, 0:4], scalar1=float(CAP),
                                        scalar2=None, op0=ALU.is_lt), "sV")
                    v.wait_ge(S_["sV"], C_["sV"])
                    inc(v.tensor_mul(gfin[:, 4 * ti:4 * ti + 4], g4[:, 4 * ti:4 * ti + 4],
                                     wsc[:, 4:8]), "sV")
                    inc(v.tensor_scalar_add(w4[:, 0:4], w4[:, 0:4], BIGIDX), "sV")
                    v.wait_ge(S_["sV"], C_["sV"])
                    inc(v.scalar_tensor_tensor(out=w4[:, 0:4], in0=wsc[:, 4:8], scalar=-BIGIDX,
                                               in1=w4[:, 0:4], op0=ALU.mult, op1=ALU.add), "sV")
                    v.wait_ge(S_["sV"], C_["sV"])
                    inc(v.tensor_copy(idx4[:, 4 * ti:4 * ti + 4], w4[:, 0:4]), "sV")
                onV(_)

            def _(g):
                wait(g, "sV", "dG")
                for ti in range(T16):
                    for k in range(4):
                        dinc(g.indirect_dma_start(
                            out=xe_dram, out_offset=IndirectOffsetOnAxis(
                                ap=idx4[:, 4 * ti + k:4 * ti + k + 1], axis=0),
                            in_=p0v[:, ti, :], in_offset=None,
                            bounds_check=bndreg[0], oob_is_err=False), "dG")
            onG(_)

            w1v = WB[:, 0:4096].rearrange("p (a i) -> p a i", i=NI)
            w3v = WB[:, 4096:8192].rearrange("p (a i) -> p a i", i=NI)
            w2v = HB[:, 0:4 * 1024].rearrange("p (b d) -> p b d", d=D)
            for e in range(NE):
                def _(g, e=e):
                    if e == 0:
                        wait(g, "sV", "dG")
                    else:
                        wait(g, "sT")
                    with g.register(f"kg{callid}_{e}") as r:
                        g.reg_load(r, kepti[0:1, e:e + 1])
                        with g.If_cmp(r, 0, "IS_GT"):
                            g.dma_start(out=w1v[:], in_=W1[e].rearrange("(a p) i -> p a i", p=128)
                                        ).then_inc(S_["dG"], 16)
                            g.dma_start(out=w3v[:], in_=W3[e].rearrange("(a p) i -> p a i", p=128)
                                        ).then_inc(S_["dG"], 16)
                            g.dma_start(out=w2v[:], in_=W2[e].rearrange("(b p) d -> p b d", p=128)
                                        ).then_inc(S_["dG"], 16)
                        with g.Else():
                            g.sem_inc(S_["dG"], 48)
                    C_["dG"] += 48
                onG(_)

                for j in range(2):
                    def _(s, e=e, j=j):
                        wait(s, "dG")
                        wait(s, "sT")
                        with s.register(f"ks{callid}_{e}_{j}") as r:
                            s.reg_load(r, kepti[0:1, e:e + 1])
                            with s.If_cmp(r, 512 * j, "IS_GT"):
                                for c in range(DCH):
                                    s.dma_start(out=qcg[:, c, :],
                                                in_=xe_dram[CAP * e + 512 * j:CAP * e + 512 * (j + 1),
                                                            128 * c:128 * (c + 1)],
                                                transpose=True).then_inc(S_["dS"], 16)
                            with s.Else():
                                s.sem_inc(S_["dS"], 16 * DCH)
                        C_["dS"] += 16 * DCH
                    onS(_)

                    for cs in range(4):
                        base = 512 * j + 128 * cs

                        def _(t, e=e, j=j, cs=cs, base=base):
                            wait(t, "dS")
                            if not (e == 0 and j == 0 and cs == 0):
                                wait(t, "sV")
                            with t.register(f"kt{callid}_{e}_{j}_{cs}") as r:
                                t.reg_load(r, kepti[0:1, e:e + 1])
                                with t.If_cmp(r, base, "IS_GT"):
                                    for b in range(4):
                                        for a in range(DCH):
                                            t.matmul(PS[0][:, 128 * b:128 * (b + 1)],
                                                     w1v[:, a, 128 * b:128 * (b + 1)],
                                                     qcg[:, a, 128 * cs:128 * (cs + 1)],
                                                     start=(a == 0), stop=(a == DCH - 1))
                                    for b in range(4):
                                        for a in range(DCH):
                                            mm = t.matmul(PS[1][:, 128 * b:128 * (b + 1)],
                                                          w3v[:, a, 128 * b:128 * (b + 1)],
                                                          qcg[:, a, 128 * cs:128 * (cs + 1)],
                                                          start=(a == 0), stop=(a == DCH - 1))
                                    mm.then_inc(S_["sT"], 1)
                                with t.Else():
                                    t.sem_inc(S_["sT"], 1)
                            C_["sT"] += 1
                        onT(_)

                        def _(s, e=e, base=base):
                            wait(s, "sT")
                            with s.register(f"ksl{callid}_{e}_{base}") as r:
                                s.reg_load(r, kepti[0:1, e:e + 1])
                                with s.If_cmp(r, base, "IS_GT"):
                                    s.activation(out=SC2[:, 0:512], in_=PS[0][:, 0:512],
                                                 func=AF.Silu).then_inc(S_["sS"], 1)
                                with s.Else():
                                    s.sem_inc(S_["sS"], 1)
                            C_["sS"] += 1
                        onS(_)

                        def _(v, e=e, base=base):
                            wait(v, "sS")
                            with v.register(f"kv{callid}_{e}_{base}") as r:
                                v.reg_load(r, kepti[0:1, e:e + 1])
                                with v.If_cmp(r, base, "IS_GT"):
                                    v.tensor_mul(OF[:, 0:512], SC2[:, 0:512],
                                                 PS[1][:, 0:512]).then_inc(S_["sV"], 1)
                                with v.Else():
                                    v.sem_inc(S_["sV"], 1)
                            C_["sV"] += 1
                        onV(_)

                        def _(t, e=e, base=base):
                            wait(t, "sV")
                            with t.register(f"ky{callid}_{e}_{base}") as r:
                                t.reg_load(r, kepti[0:1, e:e + 1])
                                with t.If_cmp(r, base, "IS_GT"):
                                    for dd in range(2):
                                        for b in range(4):
                                            mm = t.matmul(PS[4 + dd][:, 0:512],
                                                          OF[:, 128 * b:128 * (b + 1)],
                                                          w2v[:, b, 512 * dd:512 * (dd + 1)],
                                                          start=(b == 0), stop=(b == 3))
                                    mm.then_inc(S_["sT"], 1)
                                with t.Else():
                                    t.sem_inc(S_["sT"], 1)
                            C_["sT"] += 1
                        onT(_)

                        def _(v, e=e, base=base):
                            wait(v, "sT")
                            with v.register(f"kye{callid}_{e}_{base}") as r:
                                v.reg_load(r, kepti[0:1, e:e + 1])
                                with v.If_cmp(r, base, "IS_GT"):
                                    v.tensor_copy(SC2[:, 1024:1536], PS[4][:, 0:512])
                                    v.tensor_copy(SC2[:, 1536:2048],
                                                  PS[5][:, 0:512]).then_inc(S_["sV"], 1)
                                with v.Else():
                                    v.sem_inc(S_["sV"], 1)
                            C_["sV"] += 1
                        onV(_)

                        def _(y, e=e, base=base):
                            wait(y, "sV")
                            with y.register(f"kst{callid}_{e}_{base}") as r:
                                y.reg_load(r, kepti[0:1, e:e + 1])
                                with y.If_cmp(r, base, "IS_GT"):
                                    y.dma_start(out=ye_dram[CAP * e + base:CAP * e + base + 128, :],
                                                in_=SC2[:, 1024:2048]).then_inc(S_["dY"], 16)
                                with y.Else():
                                    y.sem_inc(S_["dY"], 16)
                            C_["dY"] += 16
                        onY(_)

            for ti in range(T16):
                for half in range(2):
                    def _(g, ti=ti, half=half):
                        if ti == 0 and half == 0:
                            wait(g, "dY")
                        else:
                            wait(g, "sV")
                        for k in (2 * half, 2 * half + 1):
                            dinc(g.indirect_dma_start(
                                out=gbv[:, k - 2 * half, :], out_offset=None,
                                in_=ye_dram, in_offset=IndirectOffsetOnAxis(
                                    ap=idx4[:, 4 * ti + k:4 * ti + k + 1], axis=0),
                                bounds_check=bndreg[0], oob_is_err=False), "dG")
                    onG(_)

                    def _(v, ti=ti, half=half):
                        wait(v, "dG")
                        if half == 0:
                            inc(v.tensor_scalar(out=SC2[:, 0:1024], in0=gbv[:, 0, :],
                                                scalar1=gfin[:, 4 * ti:4 * ti + 1], scalar2=None,
                                                op0=ALU.mult), "sV")
                            v.wait_ge(S_["sV"], C_["sV"])
                            inc(v.scalar_tensor_tensor(out=SC2[:, 0:1024], in0=gbv[:, 1, :],
                                                       scalar=gfin[:, 4 * ti + 1:4 * ti + 2],
                                                       in1=SC2[:, 0:1024], op0=ALU.mult,
                                                       op1=ALU.add), "sV")
                        else:
                            for k in (2, 3):
                                inc(v.scalar_tensor_tensor(out=SC2[:, 0:1024], in0=gbv[:, k - 2, :],
                                                           scalar=gfin[:, 4 * ti + k:4 * ti + k + 1],
                                                           in1=SC2[:, 0:1024], op0=ALU.mult,
                                                           op1=ALU.add), "sV")
                                v.wait_ge(S_["sV"], C_["sV"])
                            inc(v.tensor_add(rv[:, ti, :], rv[:, ti, :], SC2[:, 0:1024]), "sV")
                    onV(_)

            sw1_v = WB[:, 0:4096].rearrange("p (a i) -> p a i", i=NI)
            sw3_v = WB[:, 4096:8192].rearrange("p (a i) -> p a i", i=NI)
            sw2_v = HB[:, 0:4 * 1024].rearrange("p (b d) -> p b d", d=D)

            def _(g):
                wait(g, "sT")
                dinc(g.dma_start(out=sw1_v[:], in_=sw1T.rearrange("(a p) i -> p a i", p=128)), "dG")
                dinc(g.dma_start(out=sw3_v[:], in_=sw3T.rearrange("(a p) i -> p a i", p=128)), "dG")
                dinc(g.dma_start(out=sw2_v[:], in_=sw2T.rearrange("(b p) d -> p b d", p=128)), "dG")
            onG(_)

            for tg in range(4):
                for ic in range(4):
                    def _(t, tg=tg, ic=ic):
                        if tg == 0 and ic == 0:
                            wait(t, "dG", "sV")
                        if ic == 0 and tg > 0:
                            wait(t, "sV")
                        if ic >= 2:
                            wait(t, "sV")
                        b0 = PS[ic % 2]
                        b1 = PS[2 + ic % 2]
                        for a in range(DCH):
                            t.matmul(b0[:, 0:512], sw1_v[:, a, :][:, 128 * ic:128 * (ic + 1)],
                                     p1v[:, a, 512 * tg:512 * (tg + 1)],
                                     start=(a == 0), stop=(a == DCH - 1))
                        for a in range(DCH):
                            mm = t.matmul(b1[:, 0:512], sw3_v[:, a, :][:, 128 * ic:128 * (ic + 1)],
                                          p1v[:, a, 512 * tg:512 * (tg + 1)],
                                          start=(a == 0), stop=(a == DCH - 1))
                        inc(mm, "sT")
                    onT(_)

                    def _(s, tg=tg, ic=ic):
                        wait(s, "sT")
                        inc(s.activation(out=SC2[:, 512 * (ic % 2):512 * (ic % 2) + 512],
                                         in_=PS[ic % 2][:, 0:512], func=AF.Silu), "sS")
                    onS(_)

                    def _(v, tg=tg, ic=ic):
                        wait(v, "sS")
                        inc(v.tensor_mul(ofv[:, ic, :], SC2[:, 512 * (ic % 2):512 * (ic % 2) + 512],
                                         PS[2 + ic % 2][:, 0:512]), "sV")
                    onV(_)

                for tj in range(4):
                    ti = 4 * tg + tj

                    def _(t, tg=tg, tj=tj, ti=ti):
                        wait(t, "sV")
                        for dd in range(2):
                            for ic in range(4):
                                mm = t.matmul(PS[4 + dd][:, 0:512],
                                              ofv[:, ic, 128 * tj:128 * (tj + 1)],
                                              sw2_v[:, ic, 512 * dd:512 * (dd + 1)],
                                              start=(ic == 0), stop=(ic == 3))
                        inc(mm, "sT")
                    onT(_)

                    def _(v, ti=ti):
                        wait(v, "sT")
                        inc(v.tensor_add(rv[:, ti, 0:512], rv[:, ti, 0:512], PS[4][:, 0:512]), "sV")
                        inc(v.tensor_add(rv[:, ti, 512:1024], rv[:, ti, 512:1024],
                                         PS[5][:, 0:512]), "sV")
                    onV(_)

        def attn_b():
            ln_phase(with_xn=False)
            p0c = P0[:].rearrange("p (c t) -> p c t", t=NTOK)
            wqs = QC[:, 0:1024].rearrange("p (a e) -> p a e", e=128)
            wks = QC[:, 1024:2048].rearrange("p (a e) -> p a e", e=128)
            wvs = QC[:, 2048:3072].rearrange("p (a e) -> p a e", e=128)
            wo_v = WB[:, 0:8 * 1024].rearrange("p (a e) -> p a e", e=D)

            vaw = HB[:, 4096:4096 + 2 * 16 * 65].rearrange("p (h t a) -> p h t a", t=T16, a=65)

            def _(g):
                wait(g, "sT")
                dinc(g.dma_start(out=wo_v[:], in_=bwoT.rearrange("(a p) e -> p a e", p=128)), "dG")
            onG(_)

            for hp in range(8):
                def _(g, hp=hp):
                    if hp == 0:
                        wait(g, "dS", "sS")
                    wait(g, "sT", "sV")
                    dinc(g.dma_start(out=wqs[:], in_=bwqT[:, 128 * hp:128 * (hp + 1)]
                                     .rearrange("(a p) e -> p a e", p=128)), "dG")
                    dinc(g.dma_start(out=wks[:], in_=bwkT[:, 128 * hp:128 * (hp + 1)]
                                     .rearrange("(a p) e -> p a e", p=128)), "dG")
                    dinc(g.dma_start(out=wvs[:], in_=bwvT[:, 128 * hp:128 * (hp + 1)]
                                     .rearrange("(a p) e -> p a e", p=128)), "dG")
                    inc(g.memset(vaw[:, :, :, 64:65], 1.0), "sG")
                onG(_)

                for which, dstslot, wv_ in (("q", 0, wqs), ("k", 1, wks)):
                    for tg in range(4):
                        def _(t, which=which, tg=tg, wv_=wv_):
                            if which == "q" and tg == 0:
                                wait(t, "dG", "sG")
                            wait(t, "sV")
                            bank = PS[tg % 2]
                            for a in range(DCH):
                                mm = t.matmul(bank[:, 0:512], wv_[:, a, :],
                                              p1v[:, a, 512 * tg:512 * (tg + 1)],
                                              start=(a == 0), stop=(a == DCH - 1))
                            inc(mm, "sT")
                        onT(_)

                        def _(v, dstslot=dstslot, tg=tg):
                            wait(v, "sT")
                            inc(v.tensor_copy(hbv[:, dstslot, 512 * tg:512 * (tg + 1)],
                                              PS[tg % 2][:, 0:512]), "sV")
                        onV(_)

                for ti in range(T16):
                    def _(t, ti=ti):
                        if ti >= 2:
                            wait(t, "sV")
                        bank = PS[2 + ti % 2]
                        for a in range(DCH):
                            mm = t.matmul(bank[:, 0:128], p1v[:, a, 128 * ti:128 * (ti + 1)],
                                          wvs[:, a, :], start=(a == 0), stop=(a == DCH - 1))
                        inc(mm, "sT")
                    onT(_)

                    def _(v, ti=ti):
                        wait(v, "sT")
                        inc(v.tensor_copy(vaw[:, 0, ti, 0:64], PS[2 + ti % 2][:, 0:64]), "sV")
                        inc(v.tensor_copy(vaw[:, 1, ti, 0:64], PS[2 + ti % 2][:, 64:128]), "sV")
                    onV(_)

                for which, slot, col in (("q", 0, 4), ("k", 1, 5)):
                    def _(s, slot=slot):
                        wait(s, "sV")
                        if which == "k":
                            wait(s, "sV")
                        inc(s.activation(out=SC2[:, 0:2048], in_=hbv[:, slot, :],
                                         func=AF.Square), "sS")
                    onS(_)

                    def _(v, col=col):
                        inc(v.memset(scal[0:1, col:col + 1], 0.0), "sV")
                    onV(_)

                    for tg in range(4):
                        def _(t, tg=tg):
                            wait(t, "sS", "sV")
                            inc(t.matmul(PS[6][0:1, 0:512], c_onescol[:],
                                         SC2[:, 512 * tg:512 * (tg + 1)],
                                         start=True, stop=True), "sT")
                        onT(_)

                        def _(v, col=col):
                            wait(v, "sT")
                            inc(v.tensor_reduce(out=scal[0:1, 6:7], in_=PS[6][0:1, 0:512],
                                                axis=AX.X, op=ALU.max), "sV")
                            v.wait_ge(S_["sV"], C_["sV"])
                            inc(v.tensor_tensor(out=scal[0:1, col:col + 1],
                                                in0=scal[0:1, col:col + 1],
                                                in1=scal[0:1, 6:7], op=ALU.max), "sV")
                        onV(_)

                def _(v):
                    wait(v, "sV")
                    inc(v.tensor_mul(scal[0:1, 6:7], scal[0:1, 4:5], scal[0:1, 5:6]), "sV")
                onV(_)

                def _(s):
                    wait(s, "sV")
                    inc(s.activation(out=scal[0:1, 7:8], in_=scal[0:1, 6:7], func=AF.Sqrt,
                                     scale=1.0 / 64.0), "sS")
                    s.wait_ge(S_["sS"], C_["sS"])
                    inc(s.mul(out=scal[0:1, 6:7], in_=scal[0:1, 7:8], mul=-1.0), "sS")
                onS(_)

                def _(t):
                    wait(t, "sS")
                    inc(t.matmul(PS[6][:, 0:1], c_onesrow[:], scal[0:1, 6:7],
                                 start=True, stop=True), "sT")
                onT(_)

                def _(v):
                    wait(v, "sT")
                    inc(v.tensor_copy(negb[:, 0:1], PS[6][:, 0:1]), "sV")
                onV(_)

                for h2 in range(2):
                    for g in range(4):
                        pvb = PS[3 + h2]
                        nkj = 4 * g + 4
                        for kj in range(nkj):
                            seq = kj
                            sb_ = PS[seq % 2]
                            slot = seq % 4

                            def _(t, h2=h2, g=g, kj=kj, sb_=sb_):
                                if kj == 0:
                                    wait(t, "sV")
                                wait(t, "sS")
                                inc(t.matmul(sb_[:, 0:512],
                                             hbv[:, 1, :][64 * h2:64 * h2 + 64,
                                                          128 * kj:128 * (kj + 1)],
                                             hbv[:, 0, :][64 * h2:64 * h2 + 64,
                                                          512 * g:512 * (g + 1)],
                                             start=True, stop=True), "sT")
                            onT(_)

                            diag = kj >= 4 * g

                            def _(s, sb_=sb_, slot=slot):
                                wait(s, "sT", "sG")
                                inc(s.activation(out=ofv[:, slot, :], in_=sb_[:, 0:512],
                                                 func=AF.Exp, bias=negb[:, 0:1],
                                                 scale=0.125), "sS")
                            onS(_)

                            if diag:
                                o = kj - 4 * g

                                def _(gp, slot=slot, o=o):
                                    wait(gp, "sS")
                                    inc(gp.affine_select(
                                        out=ofv[:, slot, :], in_=ofv[:, slot, :],
                                        pattern=[[1, 512]], compare_op=ALU.is_ge,
                                        fill=0.0, base=-128 * o, channel_multiplier=-1), "sG")
                                onG(_)

                            def _(t, h2=h2, kj=kj, slot=slot, diag=diag, nkj=nkj, pvb=pvb):
                                wait(t, "sS")
                                if diag:
                                    wait(t, "sG")
                                inc(t.matmul(pvb[0:65, 0:512], vaw[:, h2, kj, :],
                                             ofv[:, slot, :],
                                             start=(kj == 0), stop=(kj == nkj - 1)), "sT")
                            onT(_)

                        def _(v):
                            wait(v, "sT")
                            inc(v.tensor_copy(srow_s, pvb[64:65, 0:512]), "sV")
                            v.wait_ge(S_["sV"], C_["sV"])
                            inc(v.reciprocal(srow_r, srow_s), "sV")
                        onV(_)

                        def _(t):
                            wait(t, "sV")
                            inc(t.matmul(PS[2][0:64, 0:512], c_onesrow[0:1, 0:64], srow_r,
                                         start=True, stop=True), "sT")
                        onT(_)

                        def _(v, h2=h2, g=g, hp=hp, pvb=pvb):
                            wait(v, "sT")
                            inc(v.tensor_copy(rbc, PS[2][0:64, 0:512]), "sV")
                            v.wait_ge(S_["sV"], C_["sV"])
                            dst = p0c[:, hp, 512 * g:512 * (g + 1)][64 * h2:64 * h2 + 64, :]
                            inc(v.tensor_mul(dst, pvb[0:64, 0:512], rbc), "sV")
                        onV(_)

            for ti in range(T16):
                for dd in range(2):
                    def _(t, ti=ti, dd=dd):
                        if ti == 0 and dd == 0:
                            wait(t, "sV", "dG")
                        if ti >= 1 and dd == 0:
                            wait(t, "sV")
                        for c in range(DCH):
                            mm = t.matmul(PS[dd][:, 0:512],
                                          p0c[:, c, 128 * ti:128 * (ti + 1)],
                                          wo_v[:, c, 512 * dd:512 * (dd + 1)],
                                          start=(c == 0), stop=(c == DCH - 1))
                        inc(mm, "sT")
                    onT(_)

                def _(v, ti=ti):
                    wait(v, "sT")
                    inc(v.tensor_add(rv[:, ti, 0:512], rv[:, ti, 0:512], PS[0][:, 0:512]), "sV")
                    inc(v.tensor_add(rv[:, ti, 512:1024], rv[:, ti, 512:1024], PS[1][:, 0:512]), "sV")
                onV(_)

        attn_a()
        if stage in ("f1", "b", "full"):
            moe_ffn(0)
        if stage in ("b", "full"):
            attn_b()
        if stage == "full":
            moe_ffn(1)

        if dbg:
            dP0 = nc.dram_tensor("dP0", [128, T16 * D], F32, kind="ExternalOutput").ap()
            dP1 = nc.dram_tensor("dP1", [128, T16 * D], F32, kind="ExternalOutput").ap()
            dHB = nc.dram_tensor("dHB", [128, 6272], F32, kind="ExternalOutput").ap()
            dSC = nc.dram_tensor("dSC", [128, 2048], F32, kind="ExternalOutput").ap()
            dsc = nc.dram_tensor("dsc", [128, 8], F32, kind="ExternalOutput").ap()
            dnb = nc.dram_tensor("dnb", [128, 2], F32, kind="ExternalOutput").ap()
            dsp0 = nc.dram_tensor("dsp0", [NTOK, D], F32, kind="ExternalOutput").ap()
            dsp1 = nc.dram_tensor("dsp1", [NTOK, D], F32, kind="ExternalOutput").ap()
            didx = nc.dram_tensor("didx", [128, T16 * 4], I32, kind="ExternalOutput").ap()
            dgf = nc.dram_tensor("dgf", [128, T16 * 4], F32, kind="ExternalOutput").ap()
            dg4 = nc.dram_tensor("dg4", [128, T16 * 4], F32, kind="ExternalOutput").ap()
            dmif = nc.dram_tensor("dmif", [128, T16 * 4], F32, kind="ExternalOutput").ap()
            dkept = nc.dram_tensor("dkept", [1, NE], F32, kind="ExternalOutput").ap()
            dcnt = nc.dram_tensor("dcnt", [1, NE], F32, kind="ExternalOutput").ap()
            doff = nc.dram_tensor("doff", [1, (T16 + 1) * NE], F32, kind="ExternalOutput").ap()
            dsg = nc.dram_tensor("dsg", [128, T16 * NE], F32, kind="ExternalOutput").ap()

            def _(g):
                wait(g, "sV", "sS", "sT", "sG", "dS", "dG", "dY")
                dinc(g.dma_start(out=dP0, in_=P0[:]), "dG")
                dinc(g.dma_start(out=dP1, in_=P1[:]), "dG")
                dinc(g.dma_start(out=dHB, in_=HB[:]), "dG")
                dinc(g.dma_start(out=dSC, in_=SC2[:]), "dG")
                dinc(g.dma_start(out=dsc, in_=scal[:]), "dG")
                dinc(g.dma_start(out=dnb, in_=negb[:]), "dG")
                dinc(g.dma_start(out=dsp0, in_=sp0), "dG")
                dinc(g.dma_start(out=dsp1, in_=sp1), "dG")
                dinc(g.dma_start(out=didx, in_=idx4[:]), "dG")
                dinc(g.dma_start(out=dgf, in_=gfin[:]), "dG")
                dinc(g.dma_start(out=dg4, in_=g4[:]), "dG")
                dinc(g.dma_start(out=dmif, in_=mif[:]), "dG")
                dinc(g.dma_start(out=dkept, in_=keptf[:]), "dG")
                dinc(g.dma_start(out=dcnt, in_=cntrow[:]), "dG")
                dinc(g.dma_start(out=doff, in_=offrow[:]), "dG")
                dinc(g.dma_start(out=dsg, in_=sgate[:]), "dG")
            onG(_)

        def _(y):
            wait(y, "sV", "sS", "sT", "sG", "dS", "dG")
            dinc(y.dma_start(out=yout.rearrange("(t p) d -> p t d", p=128), in_=rv[:]), "dY")
            y.wait_ge(S_["dY"], C_["dY"])
        onY(_)

    p.ctx.close()
    return p


def _stage_inputs(inputs, core):
    f32 = np.float32
    bf = ml_dtypes.bfloat16
    x = np.ascontiguousarray(np.asarray(inputs["L0_attn_a_x"])[core], f32)

    def t(a):
        return np.ascontiguousarray(np.asarray(a))

    def tt(a):
        return np.ascontiguousarray(np.asarray(a).T)

    qs = np.asarray(inputs["L0_attn_a_q_scale"], f32)
    ks = np.asarray(inputs["L0_attn_a_k_scale"], f32)
    nkv = np.asarray(inputs["L0_attn_a_null_kv"], f32)
    d = {
        "xin": x,
        "aWq": t(inputs["L0_attn_a_Wq"]),
        "aWkv": t(inputs["L0_attn_a_Wkv"]),
        "aWout": t(inputs["L0_attn_a_Wout"]),
        "bwqT": tt(inputs["L0_attn_b_wq"]),
        "bwkT": tt(inputs["L0_attn_b_wk"]),
        "bwvT": tt(inputs["L0_attn_b_wv"]),
        "bwoT": tt(inputs["L0_attn_b_wo"]),
        "gateT": tt(inputs["L0_ffn_gate_w"]),
        "W1": t(inputs["L0_ffn_W1"]),
        "W3": t(inputs["L0_ffn_W3"]),
        "W2": t(inputs["L0_ffn_W2"]),
        "sw1T": tt(inputs["L0_ffn_sw1"]),
        "sw3T": tt(inputs["L0_ffn_sw3"]),
        "sw2T": tt(inputs["L0_ffn_sw2"]),
        "gamma_bc": np.ascontiguousarray(np.broadcast_to(
            np.asarray(inputs["L0_attn_a_gamma"], f32)[None, :], (128, D))),
        "qscale_bc": np.ascontiguousarray(np.broadcast_to(
            np.tile(qs, NH)[None, :], (128, D))),
        "kscale_bc": np.ascontiguousarray(np.broadcast_to(ks[None, :], (128, DH))),
        "qsrow": np.ascontiguousarray(qs[None, :]),
        "ksrow": np.ascontiguousarray(ks[None, :]),
        "nullk": np.ascontiguousarray(nkv[0]),
        "nullv": np.ascontiguousarray(nkv[1]),
        "biasg_bc": np.ascontiguousarray(np.broadcast_to(
            np.asarray(inputs["L0_ffn_bias"], f32)[None, :], (128, NE))),
        "prefmask": (np.arange(8) < core).astype(f32).reshape(8, 1),
        "identbf": np.eye(128, dtype=bf),
        "triUf": np.triu(np.ones((128, 128), f32), 1),
        "onesrow": np.ones((1, 128), f32),
        "onescol": np.ones((128, 1), f32),
        "eidx_bc": np.ascontiguousarray(np.broadcast_to(
            np.arange(NE, dtype=f32)[None, :], (128, NE))),
    }
    return d


_prog_cache = {}


_DBG = False
_last_results = None


def kernel(**inputs):
    global _last_results
    stage = _STAGE
    key = (stage, _DBG)
    if key not in _prog_cache:
        _prog_cache[key] = build_program(stage, dbg=_DBG)
    p = _prog_cache[key]
    in_maps = [_stage_inputs(inputs, c) for c in range(8)]
    resl = run_bass_kernel_spmd(p.nc, in_maps, core_ids=list(range(8)))
    _last_results = resl.results
    out = np.concatenate([r["yout"][None] for r in resl.results], axis=0)
    return out.astype(np.float32)


# version 04
# speedup vs baseline: 1.3143x; 1.3143x over previous
import numpy as np
import ml_dtypes
import dataclasses
from contextlib import ExitStack

import concourse.bass as bass
import concourse.mybir as mybir
from concourse.bass_utils import run_bass_kernel_spmd
from concourse.bass import IndirectOffsetOnAxis
from concourse.bass_types import AP
from concourse.expressions import RuntimeValue

F32 = mybir.dt.float32
BF16 = mybir.dt.bfloat16
I32 = mybir.dt.int32
U32 = mybir.dt.uint32
ALU = mybir.AluOpType
AF = mybir.ActivationFunctionType
AX = mybir.AxisListType

T16 = 16
NTOK = 2048
D = 1024
DCH = 8
NH = 16
DH = 64
NE = 32
CAP = 1024
NI = 512
EPS = 1e-5
BIGIDX = float(1 << 20)

STAGES = ["ln", "a", "f1", "b", "full"]
_STAGE = "full"


def bc_mid(ap2d, n):
    return dataclasses.replace(ap2d, ap=[ap2d.ap[0], [0, n], ap2d.ap[1]])


class Prog:
    def __init__(self):
        self.nc = bass.Bass("TRN2", target_bir_lowering=False, debug=False)
        self.ctx = ExitStack()
        self.cnt = {}
        self.sem = {}
        for n in ["sT", "sV", "sS", "sG", "dY", "dS", "dG", "AG"]:
            self.sem[n] = self.ctx.enter_context(self.nc.semaphore(n))
            self.cnt[n] = 0

    def inc(self, inst, name, k=1):
        inst.then_inc(self.sem[name], k)
        self.cnt[name] += k
        return inst

    def dinc(self, inst, name):
        return self.inc(inst, name, 16)

    def wait(self, eng, *names):
        for n in names:
            eng.wait_ge(self.sem[n], self.cnt[n])

    def edge(self, eng, name):
        self.inc(eng.engine_nop() if hasattr(eng, "engine_nop") else eng.nop(), name)
        eng.wait_ge(self.sem[name], self.cnt[name])

    def sb(self, name, shape, dt):
        return self.ctx.enter_context(self.nc.sbuf_tensor(name, shape, dt))

    def psum(self, name, shape, dt=F32):
        return self.ctx.enter_context(self.nc.psum_tensor(name, shape, dt))


def build_program(stage="full", dbg=False):
    p = Prog()
    nc = p.nc
    inc, dinc, wait = p.inc, p.dinc, p.wait
    S_ = p.sem
    C_ = p.cnt

    def ein(name, shape, dt):
        return nc.dram_tensor(name, shape, dt, kind="ExternalInput").ap()

    xin = ein("xin", [NTOK, D], F32)
    yout = nc.dram_tensor("yout", [NTOK, D], F32, kind="ExternalOutput").ap()

    aWq = ein("aWq", [D, D], F32)
    aWkv = ein("aWkv", [D, 128], F32)
    aWout = ein("aWout", [D, D], F32)
    bwqT = ein("bwqT", [D, D], F32)
    bwkT = ein("bwkT", [D, D], F32)
    bwvT = ein("bwvT", [D, D], F32)
    bwoT = ein("bwoT", [D, D], F32)
    gateT = ein("gateT", [D, NE], BF16)
    W1 = ein("W1", [NE, D, NI], BF16)
    W3 = ein("W3", [NE, D, NI], BF16)
    W2 = ein("W2", [NE, NI, D], BF16)
    sw1T = ein("sw1T", [D, NI], BF16)
    sw3T = ein("sw3T", [D, NI], BF16)
    sw2T = ein("sw2T", [NI, D], BF16)
    gamma_bc = ein("gamma_bc", [128, D], F32)
    qscale_bc = ein("qscale_bc", [128, D], F32)
    kscale_bc = ein("kscale_bc", [128, DH], F32)
    qsrow = ein("qsrow", [1, DH], F32)
    ksrow = ein("ksrow", [1, DH], F32)
    nullk = ein("nullk", [2, DH], F32)
    nullv = ein("nullv", [2, DH], F32)
    biasg_bc = ein("biasg_bc", [128, NE], F32)
    prefmask = ein("prefmask", [8, 1], F32)
    identbf = ein("identbf", [128, 128], BF16)
    triUf = ein("triUf", [128, 128], F32)
    onesrow = ein("onesrow", [1, 128], F32)
    onescol = ein("onescol", [128, 1], F32)
    eidx_bc = ein("eidx_bc", [128, NE], F32)

    sp0 = nc.dram_tensor("sp0", [NTOK, D], BF16, kind="Internal").ap()
    sp1 = nc.dram_tensor("sp1", [NTOK, D], BF16, kind="Internal").ap()
    xe_dram = nc.dram_tensor("xe_dram", [NE * CAP, D], BF16, kind="Internal").ap()
    ye_dram = nc.dram_tensor("ye_dram", [NE * CAP, D], F32, kind="Internal").ap()
    ag_in = nc.dram_tensor("ag_in", [1, NE], F32, kind="Internal").ap()
    ag_out = nc.dram_tensor("ag_out", [8, NE], F32, kind="Internal").ap()

    res = p.sb("res", [128, T16 * D], F32)
    P0 = p.sb("P0", [128, T16 * D], BF16)
    P1 = p.sb("P1", [128, T16 * D], BF16)
    HB = p.sb("HB", [128, 6272], BF16)
    WB = p.sb("WB", [128, 12 * 1024], BF16)
    QC = p.sb("QC", [128, 4096], BF16)
    OF = p.sb("OF", [128, 2048], BF16)
    SC2 = p.sb("SC2", [128, 2048], F32)
    GB = p.sb("GB", [128, 2048], F32)

    st12 = p.sb("st12", [128, T16, 12], F32)
    mv2 = p.sb("mv2", [128, T16, 2], F32)
    lnS = p.sb("lnS", [128, 128], F32)
    sgate = p.sb("sgate", [128, T16 * NE], F32)
    mxs = p.sb("mxs", [128, T16 * 8], F32)
    mis = p.sb("mis", [128, T16 * 8], U32)
    mif = p.sb("mif", [128, T16 * 4], F32)
    oh4 = p.sb("oh4", [128, 4 * NE], F32)
    ohw = p.sb("ohw", [128, 4 * NE], F32)
    ohT = p.sb("ohT", [128, T16 * NE], BF16)
    w4 = p.sb("w4", [128, 4], F32)
    wsc = p.sb("wsc", [128, 8], F32)
    g4 = p.sb("g4", [128, T16 * 4], F32)
    gfin = p.sb("gfin", [128, T16 * 4], F32)
    idx4 = p.sb("idx4", [128, T16 * 4], I32)
    ctile = p.sb("ctile", [1, T16 * NE], F32)
    offrow = p.sb("offrow", [1, (T16 + 1) * NE], F32)
    cntrow = p.sb("cntrow", [1, NE], F32)
    agbuf = p.sb("agbuf", [8, NE], F32)
    keptf = p.sb("keptf", [1, NE], F32)
    kepti = p.sb("kepti", [1, NE], I32)
    scal = p.sb("scal", [128, 8], F32)
    c_onesrow = p.sb("c_onesrow", [1, 128], F32)
    c_onescol = p.sb("c_onescol", [128, 1], F32)
    c_onescol_bf = p.sb("c_onescol_bf", [128, 1], BF16)
    negb = p.sb("negb", [128, 2], F32)
    qhn = p.sb("qhn", [128, NH], F32)
    qhn2 = p.sb("qhn2", [128, NH], F32)
    kno = p.sb("kno", [128, 2], F32)
    knp = p.sb("knp", [128, DH], BF16)
    knullf = p.sb("knullf", [128, 2], BF16)
    vnaug = p.sb("vnaug", [2, 65], BF16)

    srow_s = SC2[0:1, 0:512]
    srow_r = SC2[0:1, 512:1024]
    kscr = SC2[:, 1024:1024 + DH]
    qsr = SC2[0:1, 1088:1088 + DH]
    ksr = SC2[0:1, 1152:1152 + DH]
    c_kscale = SC2[:, 1216:1216 + DH]
    c_gamma = SC2[:, 0:D]
    c_biasg = SC2[:, 0:NE]
    c_eidx = SC2[:, NE:2 * NE]
    c_prefmask = SC2[0:8, 2 * NE:2 * NE + 1]
    c_qscale = GB[:, 1024:1024 + D]
    slotm = GB[:, 0:512]
    c_ident = HB[:, 4112:4112 + 128]
    c_triU = HB[:, 4240:4240 + 128]
    rbc = SC2[0:64, 1024:1536]
    kFa = HB[:, 1024:1024 + NTOK]
    vauga = HB[:, 3072 + 0:3072 + T16 * 65]

    PSD = p.psum("psd", [128, 1024])
    PSD2 = p.psum("psd2", [128, 1024])
    _PS_rest = [p.psum(f"ps{i}", [128, 512]) for i in range(4, 7)]
    PS = [PSD[:, 0:512], PSD[:, 512:1024], PSD2[:, 0:512], PSD2[:, 512:1024]] + \
         [t[:] for t in _PS_rest]
    PSDs = [PSD, PSD2]
    PST = p.psum("pst", [128, 128], BF16)

    rv = res[:].rearrange("p (t d) -> p t d", d=D)
    p0v = P0[:].rearrange("p (t d) -> p t d", d=D)
    p1v = P1[:].rearrange("p (c t) -> p c t", t=NTOK)
    qcv = QC[:].rearrange("p (r d) -> p r d", d=D)
    qcg = QC[:].rearrange("p (c t) -> p c t", t=512)
    gbv = GB[:].rearrange("p (r d) -> p r d", d=D)
    ofv = OF[:].rearrange("p (r t) -> p r t", t=512)
    vav = vauga.rearrange("p (t a) -> p t a", a=65)
    hbv = HB[:, 0:4096].rearrange("p (r t) -> p r t", t=2048)

    with nc.Block() as block:
        onY, onT, onV, onS, onG = block.sync, block.tensor, block.vector, block.scalar, block.gpsimd

        def _(y):
            dinc(y.dma_start(out=rv[:], in_=xin.rearrange("(t p) d -> p t d", p=128)), "dY")
            dinc(y.dma_start(out=qsr, in_=qsrow), "dY")
            dinc(y.dma_start(out=ksr, in_=ksrow), "dY")
            dinc(y.dma_start(out=c_onesrow[:], in_=onesrow), "dY")
            dinc(y.dma_start(out=c_onescol[:], in_=onescol), "dY")
        onY(_)

        bndreg = [None]

        def _(g):
            bndreg[0] = g.alloc_register("bnd")
            g.reg_mov(bndreg[0], NE * CAP - 1)
            inc(g.memset(GB[:], 0.0), "sG")
            g.wait_ge(S_["sG"], C_["sG"])
            zv = xe_dram.rearrange("(n p) d -> p n d", p=128)
            gz = GB[:, 0:2048].rearrange("p (n d) -> p n d", d=D)
            for i in range(NE * CAP // 256):
                dinc(g.dma_start(out=zv[:, 2 * i:2 * i + 2, :], in_=gz), "dG")
        onG(_)

        def ln_phase(with_xn):
            def _(v):
                wait(v, "dY", "sS", "sT", "sG", "dS", "dG")
                v.wait_ge(S_["sV"], C_["sV"])
                for ti in range(T16):
                    inc(v.bn_stats(st12[:, ti, 0:6], rv[:, ti, 0:512]), "sV")
                    inc(v.bn_stats(st12[:, ti, 6:12], rv[:, ti, 512:1024]), "sV")
                v.wait_ge(S_["sV"], C_["sV"])
                for ti in range(T16):
                    inc(v.bn_aggr(mv2[:, ti, :], st12[:, ti, 0:12]), "sV")
                v.wait_ge(S_["sV"], C_["sV"])
                inc(v.tensor_scalar_add(lnS[:, 0:16], mv2[:, :, 1], EPS), "sV")
                v.wait_ge(S_["sV"], C_["sV"])
                inc(v.reciprocal(lnS[:, 16:32], lnS[:, 0:16]), "sV")
            onV(_)

            def _(s):
                wait(s, "sV")
                inc(s.activation(out=lnS[:, 32:48], in_=lnS[:, 16:32], func=AF.Sqrt), "sS")
            onS(_)

            def _(v):
                wait(v, "sS")
                inc(v.scalar_tensor_tensor(out=lnS[:, 48:64], in0=mv2[:, :, 0], scalar=-1.0,
                                           in1=lnS[:, 32:48], op0=ALU.mult, op1=ALU.mult), "sV")
            onV(_)

            def _(s):
                wait(s, "sV")
                for ti in range(T16):
                    inc(s.activation(out=p0v[:, ti, :], in_=rv[:, ti, :], func=AF.Identity,
                                     bias=lnS[:, 48 + ti:49 + ti], scale=lnS[:, 32 + ti:33 + ti]), "sS")
            onS(_)

            def _(y):
                wait(y, "sS")
                dinc(y.dma_start(out=sp0.rearrange("(t p) d -> p t d", p=128), in_=p0v[:]), "dY")
            onY(_)

            def _(s):
                wait(s, "dY")
                for c in range(DCH):
                    dinc(s.dma_start(out=p1v[:, c, :], in_=sp0[:, 128 * c:128 * (c + 1)],
                                     transpose=True), "dS")
            onS(_)

            if with_xn:
                def _(v):
                    wait(v, "sS")
                    inc(v.tensor_mul(lnS[:, 64:80], lnS[:, 32:48], lnS[:, 32:48]), "sV")
                    v.wait_ge(S_["sV"], C_["sV"])
                    inc(v.tensor_mul(lnS[:, 64:80], lnS[:, 64:80], mv2[:, :, 1]), "sV")
                    v.wait_ge(S_["sV"], C_["sV"])
                    inc(v.tensor_scalar_add(lnS[:, 64:80], lnS[:, 64:80], EPS), "sV")
                    v.wait_ge(S_["sV"], C_["sV"])
                    inc(v.reciprocal(lnS[:, 80:96], lnS[:, 64:80]), "sV")
                onV(_)

                def _(s):
                    wait(s, "sV")
                    inc(s.activation(out=lnS[:, 96:112], in_=lnS[:, 80:96], func=AF.Sqrt), "sS")
                onS(_)

                def _(v):
                    wait(v, "sS")
                    inc(v.tensor_mul(lnS[:, 96:112], lnS[:, 96:112], lnS[:, 32:48]), "sV")
                    v.wait_ge(S_["sV"], C_["sV"])
                    inc(v.scalar_tensor_tensor(out=lnS[:, 112:128], in0=mv2[:, :, 0], scalar=-1.0,
                                               in1=lnS[:, 96:112], op0=ALU.mult, op1=ALU.mult), "sV")
                onV(_)

                def _(y):
                    dinc(y.dma_start(out=c_gamma, in_=gamma_bc), "dY")
                onY(_)

                xnv = sp1.rearrange("(t p) d -> p t d", p=128)
                for grp in range(8):
                    def _(s, grp=grp):
                        wait(s, "sV")
                        for j in range(2):
                            ti = grp * 2 + j
                            inc(s.activation(out=gbv[:, j, :], in_=rv[:, ti, :], func=AF.Identity,
                                             bias=lnS[:, 112 + ti:113 + ti],
                                             scale=lnS[:, 96 + ti:97 + ti]), "sS")
                    onS(_)

                    def _(v, grp=grp):
                        wait(v, "sS", "dY")
                        for j in range(2):
                            inc(v.tensor_mul(qcv[:, j, :], gbv[:, j, :], c_gamma), "sV")
                    onV(_)

                    def _(y, grp=grp):
                        wait(y, "sV")
                        dinc(y.dma_start(out=xnv[:, 2 * grp:2 * grp + 2, :],
                                         in_=qcv[:, 0:2, :]), "dY")
                    onY(_)

        def attn_a():
            ln_phase(with_xn=True)
            wkv_v = HB[:, 0:8 * 128].rearrange("p (a e) -> p a e", e=128)
            wq_v = WB[:, 0:8 * 1024].rearrange("p (a e) -> p a e", e=D)
            wout_v = WB[:, 0:8 * 1024].rearrange("p (a e) -> p a e", e=D)

            def _(g):
                dinc(g.dma_start(out=c_ident, in_=identbf), "dG")
                dinc(g.dma_start(out=c_kscale, in_=kscale_bc), "dG")
                dinc(g.dma_start(out=wkv_v[:], in_=aWkv.rearrange("(a p) e -> p a e", p=128)), "dG")
                dinc(g.dma_start(out=wq_v[:], in_=aWq.rearrange("(a p) e -> p a e", p=128)), "dG")
                inc(g.memset(knp[:], 0.0), "sG")
                inc(g.memset(vnaug[:], 1.0), "sG")
                inc(g.memset(vauga, 1.0), "sG")
                g.wait_ge(S_["sG"], C_["sG"])
                dinc(g.dma_start(out=vnaug[0:2, 0:64], in_=nullv), "dG")
                dinc(g.dma_start(out=kscr[0:2, :], in_=nullk), "dG")
            onG(_)

            def _(v):
                wait(v, "dY")
                inc(v.tensor_reduce(out=scal[0:1, 4:5], in_=qsr, axis=AX.X, op=ALU.max,
                                    apply_absolute_value=True), "sV")
                inc(v.tensor_reduce(out=scal[0:1, 5:6], in_=ksr, axis=AX.X, op=ALU.max,
                                    apply_absolute_value=True), "sV")
                v.wait_ge(S_["sV"], C_["sV"])
                inc(v.tensor_mul(scal[0:1, 6:7], scal[0:1, 4:5], scal[0:1, 5:6]), "sV")
                v.wait_ge(S_["sV"], C_["sV"])
                inc(v.tensor_scalar_mul(scal[0:1, 6:7], scal[0:1, 6:7], -8.0), "sV")
            onV(_)

            def _(t):
                wait(t, "sV")
                inc(t.matmul(PS[3][:, 0:1], c_onesrow[:], scal[0:1, 6:7], start=True, stop=True), "sT")
            onT(_)

            def _(v):
                wait(v, "sT")
                inc(v.tensor_copy(negb[:, 0:1], PS[3][:, 0:1]), "sV")
            onV(_)

            def _(s):
                wait(s, "dG")
                inc(s.activation(out=SC2[0:2, 0:64], in_=kscr[0:2, :], func=AF.Square,
                                 accum_out=kno[0:2, 0:1]), "sS")
                s.wait_ge(S_["sS"], C_["sS"])
                inc(s.activation(out=kno[0:2, 0:1], in_=kno[0:2, 0:1], func=AF.Sqrt), "sS")
            onS(_)

            def _(v):
                wait(v, "sS")
                inc(v.tensor_scalar_max(kno[0:2, 0:1], kno[0:2, 0:1], 1e-12), "sV")
                v.wait_ge(S_["sV"], C_["sV"])
                inc(v.reciprocal(kno[0:2, 1:2], kno[0:2, 0:1]), "sV")
                v.wait_ge(S_["sV"], C_["sV"])
                inc(v.tensor_scalar(out=kscr[0:2, :], in0=kscr[0:2, :], scalar1=kno[0:2, 1:2],
                                    scalar2=8.0, op0=ALU.mult, op1=ALU.mult), "sV")
                v.wait_ge(S_["sV"], C_["sV"])
                inc(v.tensor_mul(knp[0:2, :], kscr[0:2, :], c_kscale[0:2, :]), "sV")
            onV(_)

            def _(t):
                wait(t, "sV", "dY", "dG")
                t.transpose(PST[0:64, 0:128], knp[:, 0:64], c_ident)
                inc(t.transpose(PST[64:128, 0:128], knp[:, 0:64], c_ident), "sT")
            onT(_)

            def _(v):
                wait(v, "sT")
                inc(v.tensor_copy(knullf[:, 0:2], PST[:, 0:2]), "sV")
                v.wait_ge(S_["sV"], C_["sV"])
                inc(v.memset(knp[:], 0.0), "sV")
            onV(_)

            for ti in range(T16):
                def _(t, ti=ti):
                    if ti == 0:
                        wait(t, "dS", "dG", "sG", "sV")
                    if ti >= 2:
                        wait(t, "sV")
                    bank = PS[ti % 2]
                    for a in range(DCH):
                        mm = t.matmul(bank[:, 0:128], p1v[:, a, 128 * ti:128 * (ti + 1)],
                                      wkv_v[:, a, :], start=(a == 0), stop=(a == DCH - 1))
                    inc(mm, "sT")
                onT(_)

                def _(s, ti=ti):
                    wait(s, "sT")
                    bank = PS[ti % 2]
                    inc(s.activation(out=kscr[:, 0:DH], in_=bank[:, 0:DH], func=AF.Square,
                                     accum_out=kno[:, 0:1]), "sS")
                    s.wait_ge(S_["sS"], C_["sS"])
                    inc(s.activation(out=kno[:, 0:1], in_=kno[:, 0:1], func=AF.Sqrt), "sS")
                onS(_)

                def _(v, ti=ti):
                    wait(v, "sS")
                    bank = PS[ti % 2]
                    inc(v.tensor_copy(vav[:, ti, 0:64], bank[:, 64:128]), "sV")
                    inc(v.tensor_scalar_max(kno[:, 0:1], kno[:, 0:1], 1e-12), "sV")
                    v.wait_ge(S_["sV"], C_["sV"])
                    inc(v.reciprocal(kno[:, 1:2], kno[:, 0:1]), "sV")
                    v.wait_ge(S_["sV"], C_["sV"])
                    inc(v.tensor_scalar(out=kscr[:, 0:DH], in0=bank[:, 0:DH], scalar1=kno[:, 1:2],
                                        scalar2=8.0, op0=ALU.mult, op1=ALU.mult), "sV")
                    v.wait_ge(S_["sV"], C_["sV"])
                    inc(v.tensor_mul(knp[:, 0:DH], kscr[:, 0:DH], c_kscale), "sV")
                onV(_)

                def _(t, ti=ti):
                    wait(t, "sV")
                    t.transpose(PST[0:64, 0:128], knp[:, 0:64], c_ident)
                    inc(t.transpose(PST[64:128, 0:128], knp[:, 0:64], c_ident), "sT")
                onT(_)

                def _(v, ti=ti):
                    wait(v, "sT")
                    inc(v.tensor_copy(kFa[:, 128 * ti:128 * (ti + 1)], PST[:, 0:128]), "sV")
                onV(_)

            def _(s):
                wait(s, "sV", "dY")
                for c in range(DCH):
                    dinc(s.dma_start(out=p1v[:, c, :], in_=sp1[:, 128 * c:128 * (c + 1)],
                                     transpose=True), "dS")
            onS(_)

            def _(g):
                wait(g, "sV")
                dinc(g.dma_start(out=c_qscale, in_=qscale_bc), "dG")
            onG(_)

            qdv = sp0.rearrange("(t p) d -> p t d", p=128)
            for ti in range(T16):
                def _(t, ti=ti):
                    if ti == 0:
                        wait(t, "dS", "dG")
                    if ti >= 1:
                        wait(t, "sV")
                    for h2 in range(2):
                        for a in range(DCH):
                            mm = t.matmul(PS[h2][:, 0:512], p1v[:, a, 128 * ti:128 * (ti + 1)],
                                          wq_v[:, a, 512 * h2:512 * (h2 + 1)],
                                          start=(a == 0), stop=(a == DCH - 1))
                    inc(mm, "sT")
                onT(_)

                def _(v, ti=ti):
                    wait(v, "sT")
                    inc(v.tensor_copy(SC2[:, 0:512], PS[0][:, 0:512]), "sV")
                    inc(v.tensor_copy(SC2[:, 512:1024], PS[1][:, 0:512]), "sV")
                onV(_)

                def _(s, ti=ti):
                    wait(s, "sV")
                    inc(s.activation(out=GB[:, 0:1024], in_=SC2[:, 0:1024], func=AF.Square), "sS")
                onS(_)

                def _(v, ti=ti):
                    wait(v, "sS")
                    inc(v.tensor_reduce(out=qhn[:, 0:16], in_=GB[:, 0:1024].rearrange(
                        "p (h d) -> p h d", d=DH), axis=AX.X, op=ALU.add), "sV")
                onV(_)

                def _(s, ti=ti):
                    wait(s, "sV")
                    inc(s.activation(out=qhn2[:, 0:16], in_=qhn[:, 0:16], func=AF.Sqrt), "sS")
                onS(_)

                def _(v, ti=ti):
                    wait(v, "sS")
                    inc(v.tensor_scalar_max(qhn2[:, 0:16], qhn2[:, 0:16], 1e-12), "sV")
                    v.wait_ge(S_["sV"], C_["sV"])
                    inc(v.reciprocal(qhn[:, 0:16], qhn2[:, 0:16]), "sV")
                    v.wait_ge(S_["sV"], C_["sV"])
                    inc(v.tensor_tensor(out=SC2[:, 0:1024].rearrange("p (h d) -> p h d", d=DH),
                                        in0=SC2[:, 0:1024].rearrange("p (h d) -> p h d", d=DH),
                                        in1=qhn[:, 0:16].to_broadcast([128, 16, DH]),
                                        op=ALU.mult), "sV")
                    v.wait_ge(S_["sV"], C_["sV"])
                    inc(v.tensor_mul(qcv[:, ti % 4, :], SC2[:, 0:1024], c_qscale), "sV")
                onV(_)

                def _(y, ti=ti):
                    wait(y, "sV")
                    dinc(y.dma_start(out=qdv[:, ti, :], in_=qcv[:, ti % 4, :]), "dY")
                onY(_)

            def _(g):
                wait(g, "sT")
                dinc(g.dma_start(out=wout_v[:], in_=aWout.rearrange("(a p) e -> p a e", p=128)), "dG")
            onG(_)

            SEM = S_
            for g in range(4):
                nkj = 4 * g + 4
                qT, qS, qV, qG, qY = [], [], [], [], []

                w_spills = C_["dY"]
                w_prevT = C_["sT"]
                for c in range(DCH):
                    C_["dY"] += 16
                    def f(y, c=c, g=g, w_spills=w_spills, w_prevT=w_prevT):
                        if c == 0:
                            y.wait_ge(SEM["dY"], w_spills)
                            y.wait_ge(SEM["sT"], w_prevT)
                        y.dma_start(out=qcg[:, c, :],
                                    in_=sp0[512 * g:512 * (g + 1), 128 * c:128 * (c + 1)],
                                    transpose=True).then_inc(SEM["dY"], 16)
                    qY.append(f)
                v_qload = C_["dY"]

                for hg in range(8):
                    nst = 2 * nkj
                    npr = nst // 2
                    v_mm, v_exp, v_aff, v_pv = {}, {}, {}, {}
                    v_prevdiv = C_["sV"]

                    def emit_pvs(pr2):
                        for sq2 in (2 * pr2, 2 * pr2 + 1):
                            kj3, hh3 = divmod(sq2, 2)
                            slot = sq2 % 4
                            wv = [("sS", v_exp[pr2])]
                            if sq2 in v_aff:
                                wv.append(("sG", v_aff[sq2]))
                            C_["sT"] += 1
                            v_pv[sq2] = C_["sT"]
                            def f(t, kj3=kj3, hh3=hh3, slot=slot, wv=tuple(wv)):
                                for sem, val in wv:
                                    t.wait_ge(SEM[sem], val)
                                t.matmul(PS[4 + hh3][0:65, 0:512], vav[:, kj3, :],
                                         ofv[:, slot, :], start=(kj3 == 0), stop=False
                                         ).then_inc(SEM["sT"], 1)
                            qT.append(f)

                    for pr in range(npr):
                        kj = pr
                        dbl = pr % 2
                        for half in range(2):
                            seq = 2 * pr + half
                            hh = half
                            h = 2 * hg + hh
                            hb = (h % 2) * 64
                            wv = []
                            if seq == 0:
                                wv.append(("dY", v_qload))
                                wv.append(("sV", v_prevdiv))
                            if pr >= 2 and half == 0:
                                wv.append(("sS", v_exp[pr - 2]))
                            C_["sT"] += 1
                            v_mm[seq] = C_["sT"]
                            def f(t, kj=kj, h=h, hb=hb, half=half, dbl=dbl, wv=tuple(wv)):
                                for sem, val in wv:
                                    t.wait_ge(SEM[sem], val)
                                t.matmul(PSDs[dbl][:, 512 * half:512 * half + 512],
                                         kFa[hb:hb + 64, 128 * kj:128 * (kj + 1)],
                                         qcg[:, h // 2, :][hb:hb + 64, :],
                                         start=True, stop=True).then_inc(SEM["sT"], 1)
                            qT.append(f)

                        pb = pr % 2
                        wv = [("sT", v_mm[2 * pr + 1])]
                        if pr >= 2:
                            wv.append(("sT", v_pv[2 * (pr - 2) + 1]))
                            a0 = v_aff.get(2 * (pr - 2))
                            a1 = v_aff.get(2 * (pr - 2) + 1)
                            av = max([x for x in (a0, a1) if x is not None], default=None)
                            if av is not None:
                                wv.append(("sG", av))
                        C_["sS"] += 1
                        v_exp[pr] = C_["sS"]
                        def f(sc, pb=pb, dbl=dbl, wv=tuple(wv)):
                            for sem, val in wv:
                                sc.wait_ge(SEM[sem], val)
                            sc.activation(out=OF[:, 1024 * pb:1024 * pb + 1024],
                                          in_=PSDs[dbl][:, 0:1024], func=AF.Exp,
                                          bias=negb[:, 0:1], scale=1.0
                                          ).then_inc(SEM["sS"], 1)
                        qS.append(f)

                        if kj >= 4 * g:
                            o = kj - 4 * g
                            for sq2 in (2 * pr, 2 * pr + 1):
                                slot = sq2 % 4
                                C_["sG"] += 1
                                v_aff[sq2] = C_["sG"]
                                if g == 0 and kj == 0:
                                    def f(gp, slot=slot, ve=v_exp[pr]):
                                        gp.wait_ge(SEM["sS"], ve)
                                        gp.memset(ofv[:, slot, 0:64][64:128, :], 0.0)
                                        gp.affine_select(
                                            out=ofv[:, slot, 64:512],
                                            in_=ofv[:, slot, 64:512],
                                            pattern=[[1, 448]], compare_op=ALU.is_ge,
                                            fill=0.0, base=64, channel_multiplier=-1
                                        ).then_inc(SEM["sG"], 1)
                                else:
                                    def f(gp, slot=slot, o=o, ve=v_exp[pr]):
                                        gp.wait_ge(SEM["sS"], ve)
                                        gp.affine_select(
                                            out=ofv[:, slot, :], in_=ofv[:, slot, :],
                                            pattern=[[1, 512]], compare_op=ALU.is_ge,
                                            fill=0.0, base=-128 * o,
                                            channel_multiplier=-1).then_inc(SEM["sG"], 1)
                                qG.append(f)

                        if pr >= 1:
                            emit_pvs(pr - 1)
                    emit_pvs(npr - 1)

                    v_nexp = {}
                    for hh in range(2):
                        h = 2 * hg + hh
                        hb = (h % 2) * 64
                        wv = [("sS", v_exp[nst // 2 - 1])]
                        if hh > 0:
                            wv.append(("sS", v_nexp[hh - 1]))
                        C_["sT"] += 1
                        vm = C_["sT"]
                        def f(t, h=h, hb=hb, wv=tuple(wv)):
                            for sem, val in wv:
                                t.wait_ge(SEM[sem], val)
                            t.matmul(PS[6][0:2, 0:512], knullf[hb:hb + 64, 0:2],
                                     qcg[:, h // 2, :][hb:hb + 64, :],
                                     start=True, stop=True).then_inc(SEM["sT"], 1)
                        qT.append(f)
                        C_["sS"] += 1
                        v_nexp[hh] = C_["sS"]
                        def f(sc, hh=hh, vm=vm):
                            sc.wait_ge(SEM["sT"], vm)
                            sc.activation(out=ofv[:, hh, :][0:2, :], in_=PS[6][0:2, 0:512],
                                          func=AF.Exp, bias=negb[0:2, 0:1], scale=1.0
                                          ).then_inc(SEM["sS"], 1)
                        qS.append(f)
                        C_["sT"] += 1
                        v_pvs = C_["sT"]
                        def f(t, hh=hh, ve=v_nexp[hh]):
                            t.wait_ge(SEM["sS"], ve)
                            t.matmul(PS[4 + hh][0:65, 0:512], vnaug[0:2, :],
                                     ofv[:, hh, :][0:2, :], start=False, stop=True
                                     ).then_inc(SEM["sT"], 1)
                        qT.append(f)
                        v_pv[("null", hh)] = v_pvs

                    for hh in range(2):
                        h = 2 * hg + hh
                        wv = v_pv[("null", hh)]
                        v_prevchain = C_["sV"]
                        C_["sV"] += 1
                        va = C_["sV"]
                        C_["sV"] += 1
                        vb = C_["sV"]
                        def f(v, hh=hh, wv=wv, va=va, vp=v_prevchain):
                            v.wait_ge(SEM["sT"], wv)
                            v.wait_ge(SEM["sV"], vp)
                            v.tensor_copy(srow_s, PS[4 + hh][64:65, 0:512]).then_inc(SEM["sV"], 1)
                            v.wait_ge(SEM["sV"], va)
                            v.reciprocal(srow_r, srow_s).then_inc(SEM["sV"], 1)
                        qV.append(f)
                        C_["sT"] += 1
                        vm = C_["sT"]
                        def f(t, vb=vb):
                            t.wait_ge(SEM["sV"], vb)
                            t.matmul(PS[6][0:64, 0:512], c_onesrow[0:1, 0:64], srow_r,
                                     start=True, stop=True).then_inc(SEM["sT"], 1)
                        qT.append(f)
                        C_["sV"] += 1
                        vc = C_["sV"]
                        C_["sV"] += 1
                        def f(v, hh=hh, h=h, g=g, vm=vm, vc=vc):
                            v.wait_ge(SEM["sT"], vm)
                            v.tensor_copy(rbc, PS[6][0:64, 0:512]).then_inc(SEM["sV"], 1)
                            v.wait_ge(SEM["sV"], vc)
                            dst = p1v[:, h // 2, 512 * g:512 * (g + 1)][(h % 2) * 64:(h % 2) * 64 + 64, :]
                            v.tensor_mul(dst, PS[4 + hh][0:64, 0:512], rbc).then_inc(SEM["sV"], 1)
                        qV.append(f)

                def mk(fns):
                    def body(eng):
                        for fn in fns:
                            fn(eng)
                    return body
                if qY:
                    onY(mk(qY))
                if qT:
                    onT(mk(qT))
                if qS:
                    onS(mk(qS))
                if qG:
                    onG(mk(qG))
                if qV:
                    onV(mk(qV))

            for ti in range(T16):
                for dd in range(2):
                    def _(t, ti=ti, dd=dd):
                        if ti == 0 and dd == 0:
                            wait(t, "sV", "dG")
                        if ti >= 1 and dd == 0:
                            wait(t, "sV")
                        for c in range(DCH):
                            mm = t.matmul(PS[dd][:, 0:512],
                                          p1v[:, c, 128 * ti:128 * (ti + 1)],
                                          wout_v[:, c, 512 * dd:512 * (dd + 1)],
                                          start=(c == 0), stop=(c == DCH - 1))
                        inc(mm, "sT")
                    onT(_)

                def _(v, ti=ti):
                    wait(v, "sT")
                    inc(v.tensor_add(rv[:, ti, 0:512], rv[:, ti, 0:512], PS[0][:, 0:512]), "sV")
                    inc(v.tensor_add(rv[:, ti, 512:1024], rv[:, ti, 512:1024], PS[1][:, 0:512]), "sV")
                onV(_)

        def moe_ffn(callid):
            ln_phase(with_xn=False)
            gate_v = QC[:, 0:8 * NE].rearrange("p (a e) -> p a e", e=NE)

            def _(g):
                dinc(g.dma_start(out=gate_v[:], in_=gateT.rearrange("(a p) e -> p a e", p=128)), "dG")
                dinc(g.dma_start(out=c_triU, in_=triUf), "dG")
                dinc(g.dma_start(out=c_onescol_bf[:], in_=onescol), "dG")
            onG(_)

            def _(y):
                wait(y, "sV", "sS")
                dinc(y.dma_start(out=c_biasg, in_=biasg_bc), "dY")
                dinc(y.dma_start(out=c_eidx, in_=eidx_bc), "dY")
                dinc(y.dma_start(out=c_prefmask, in_=prefmask), "dY")
            onY(_)

            def _(t):
                wait(t, "dS", "dG", "sS", "sV")
                for ti in range(T16):
                    for a in range(DCH):
                        mm = t.matmul(PS[2][:, 32 * ti:32 * (ti + 1)],
                                      p1v[:, a, 128 * ti:128 * (ti + 1)], gate_v[:, a, :],
                                      start=(a == 0), stop=(a == DCH - 1))
                inc(mm, "sT")
            onT(_)

            def _(s):
                wait(s, "sT")
                inc(s.activation(out=sgate[:], in_=PS[2][:, 0:512], func=AF.Sigmoid), "sS")
            onS(_)

            def _(v):
                wait(v, "sS", "dY")
                inc(v.tensor_tensor(out=slotm.rearrange("p (t e) -> p t e", e=NE),
                                    in0=sgate[:].rearrange("p (t e) -> p t e", e=NE),
                                    in1=bc_mid(c_biasg, T16), op=ALU.add), "sV")
                v.wait_ge(S_["sV"], C_["sV"])
                for ti in range(T16):
                    inc(v.max(out=mxs[:, 8 * ti:8 * ti + 8], in_=slotm[:, 32 * ti:32 * (ti + 1)]), "sV")
                v.wait_ge(S_["sV"], C_["sV"])
                for ti in range(T16):
                    inc(v.max_index(out=mis[:, 8 * ti:8 * ti + 8], in_max=mxs[:, 8 * ti:8 * ti + 8],
                                    in_values=slotm[:, 32 * ti:32 * (ti + 1)]), "sV")
                v.wait_ge(S_["sV"], C_["sV"])
                inc(v.tensor_copy(mif[:].rearrange("p (t k) -> p t k", k=4),
                                  mis[:].rearrange("p (t k) -> p t k", k=8)[:, :, 0:4]), "sV")
            onV(_)

            for ti in range(T16):
                def _(v, ti=ti):
                    wait(v, "sV")
                    for k in range(4):
                        inc(v.tensor_scalar(out=oh4[:, 32 * k:32 * (k + 1)], in0=c_eidx,
                                            scalar1=mif[:, 4 * ti + k:4 * ti + k + 1], scalar2=None,
                                            op0=ALU.is_equal), "sV")
                    v.wait_ge(S_["sV"], C_["sV"])
                    inc(v.tensor_add(ohw[:, 0:32], oh4[:, 0:32], oh4[:, 32:64]), "sV")
                    inc(v.tensor_add(ohw[:, 32:64], oh4[:, 64:96], oh4[:, 96:128]), "sV")
                    v.wait_ge(S_["sV"], C_["sV"])
                    inc(v.tensor_add(ohT[:, 32 * ti:32 * (ti + 1)], ohw[:, 0:32], ohw[:, 32:64]), "sV")
                    v.wait_ge(S_["sV"], C_["sV"])
                    inc(v.tensor_tensor(out=ohw[:].rearrange("p (k e) -> p k e", e=NE),
                                        in0=oh4[:].rearrange("p (k e) -> p k e", e=NE),
                                        in1=bc_mid(sgate[:, 32 * ti:32 * (ti + 1)], 4),
                                        op=ALU.mult), "sV")
                    v.wait_ge(S_["sV"], C_["sV"])
                    inc(v.tensor_reduce(out=w4[:, 0:4], in_=ohw[:].rearrange("p (k e) -> p k e", e=NE),
                                        axis=AX.X, op=ALU.add), "sV")
                    v.wait_ge(S_["sV"], C_["sV"])
                    inc(v.tensor_reduce(out=wsc[:, 0:1], in_=w4[:, 0:4], axis=AX.X, op=ALU.add), "sV")
                    v.wait_ge(S_["sV"], C_["sV"])
                    inc(v.tensor_scalar_max(wsc[:, 1:2], wsc[:, 0:1], 1e-12), "sV")
                    v.wait_ge(S_["sV"], C_["sV"])
                    inc(v.reciprocal(wsc[:, 2:3], wsc[:, 1:2]), "sV")
                    v.wait_ge(S_["sV"], C_["sV"])
                    inc(v.tensor_scalar(out=g4[:, 4 * ti:4 * ti + 4], in0=w4[:, 0:4],
                                        scalar1=wsc[:, 2:3], scalar2=None, op0=ALU.mult), "sV")
                onV(_)

            for ti in range(T16):
                def _(t, ti=ti):
                    if ti == 0:
                        wait(t, "sV", "sS")
                    mm1 = t.matmul(PS[3][0:1, 32 * ti:32 * (ti + 1)], c_onescol_bf[:],
                                   ohT[:, 32 * ti:32 * (ti + 1)], start=True, stop=True)
                    mm2 = t.matmul(PS[4][0:1, 0:32], c_onescol_bf[:], ohT[:, 32 * ti:32 * (ti + 1)],
                                   start=(ti == 0), stop=(ti == T16 - 1))
                    if ti == T16 - 1:
                        inc(mm1, "sT")
                        inc(mm2, "sT")
                onT(_)

            def _(v):
                wait(v, "sT")
                inc(v.tensor_copy(ctile[:], PS[3][0:1, 0:512]), "sV")
                inc(v.tensor_copy(cntrow[:], PS[4][0:1, 0:32]), "sV")
            onV(_)

            def _(y):
                wait(y, "sV")
                dinc(y.dma_start(out=ag_in, in_=cntrow[:]), "dY")
            onY(_)

            def _(g):
                wait(g, "dY")
                inc(g.collective_compute("AllGather", ALU.bypass,
                                         replica_groups=[[0, 1, 2, 3, 4, 5, 6, 7]],
                                         ins=[ag_in], outs=[ag_out]), "AG")
            onG(_)

            def _(y):
                wait(y, "AG")
                dinc(y.dma_start(out=agbuf[:], in_=ag_out), "dY")
            onY(_)

            def _(t):
                wait(t, "dY", "sV")
                inc(t.matmul(PS[4][0:1, 0:32], c_prefmask, agbuf[:], start=True, stop=True), "sT")
            onT(_)

            ocrow = SC2[0:1, 96:128]
            obc = SC2[:, 96:128]

            def _(v):
                wait(v, "sT")
                inc(v.tensor_copy(ocrow, PS[4][0:1, 0:32]), "sV")
                inc(v.memset(offrow[0:1, 0:32], 0.0), "sV")
                v.wait_ge(S_["sV"], C_["sV"])
                for ti in range(T16):
                    inc(v.tensor_add(offrow[0:1, 32 * (ti + 1):32 * (ti + 2)],
                                     offrow[0:1, 32 * ti:32 * (ti + 1)],
                                     ctile[0:1, 32 * ti:32 * (ti + 1)]), "sV")
                    v.wait_ge(S_["sV"], C_["sV"])
                inc(v.tensor_scalar(out=keptf[:], in0=ocrow, scalar1=-1.0,
                                    scalar2=float(CAP), op0=ALU.mult, op1=ALU.add), "sV")
                v.wait_ge(S_["sV"], C_["sV"])
                inc(v.tensor_scalar_max(keptf[:], keptf[:], 0.0), "sV")
                v.wait_ge(S_["sV"], C_["sV"])
                inc(v.tensor_tensor(out=keptf[:], in0=keptf[:], in1=cntrow[:], op=ALU.min), "sV")
                v.wait_ge(S_["sV"], C_["sV"])
                inc(v.tensor_copy(kepti[:], keptf[:]), "sV")
            onV(_)

            def _(t):
                wait(t, "sV")
                inc(t.matmul(PS[5][:, 0:32], c_onesrow[:], ocrow, start=True, stop=True), "sT")
            onT(_)

            def _(v):
                wait(v, "sT")
                inc(v.tensor_copy(obc, PS[5][:, 0:32]), "sV")
            onV(_)

            for ti in range(T16):
                def _(t, ti=ti):
                    if ti == 0:
                        wait(t, "sV")
                    mm1 = t.matmul(PS[2][:, 32 * ti:32 * (ti + 1)], c_triU,
                                   ohT[:, 32 * ti:32 * (ti + 1)], start=True, stop=False)
                    mm2 = t.matmul(PS[2][:, 32 * ti:32 * (ti + 1)], c_onesrow[:],
                                   offrow[0:1, 32 * ti:32 * (ti + 1)], start=False, stop=True)
                    if ti == T16 - 1:
                        inc(mm2, "sT")
                onT(_)

            def _(v):
                wait(v, "sT")
                inc(v.scalar_tensor_tensor(out=slotm.rearrange("p (t e) -> p t e", e=NE),
                                           in0=bc_mid(c_eidx, T16), scalar=float(CAP),
                                           in1=PS[2][:, 0:512].rearrange("p (t e) -> p t e", e=NE),
                                           op0=ALU.mult, op1=ALU.add), "sV")
            onV(_)

            for ti in range(T16):
                def _(v, ti=ti):
                    wait(v, "sV")
                    for k in range(4):
                        inc(v.tensor_scalar(out=oh4[:, 32 * k:32 * (k + 1)], in0=c_eidx,
                                            scalar1=mif[:, 4 * ti + k:4 * ti + k + 1], scalar2=None,
                                            op0=ALU.is_equal), "sV")
                    v.wait_ge(S_["sV"], C_["sV"])
                    inc(v.tensor_tensor(out=ohw[:].rearrange("p (k e) -> p k e", e=NE),
                                        in0=oh4[:].rearrange("p (k e) -> p k e", e=NE),
                                        in1=bc_mid(slotm[:, 32 * ti:32 * (ti + 1)], 4),
                                        op=ALU.mult), "sV")
                    v.wait_ge(S_["sV"], C_["sV"])
                    inc(v.tensor_reduce(out=w4[:, 0:4], in_=ohw[:].rearrange("p (k e) -> p k e", e=NE),
                                        axis=AX.X, op=ALU.add), "sV")
                    v.wait_ge(S_["sV"], C_["sV"])
                    inc(v.tensor_tensor(out=ohw[:].rearrange("p (k e) -> p k e", e=NE),
                                        in0=oh4[:].rearrange("p (k e) -> p k e", e=NE),
                                        in1=bc_mid(obc, 4), op=ALU.mult), "sV")
                    v.wait_ge(S_["sV"], C_["sV"])
                    inc(v.tensor_reduce(out=wsc[:, 0:4], in_=ohw[:].rearrange("p (k e) -> p k e", e=NE),
                                        axis=AX.X, op=ALU.add), "sV")
                    v.wait_ge(S_["sV"], C_["sV"])
                    inc(v.scalar_tensor_tensor(out=wsc[:, 0:4], in0=mif[:, 4 * ti:4 * ti + 4],
                                               scalar=-float(CAP), in1=wsc[:, 0:4],
                                               op0=ALU.mult, op1=ALU.add), "sV")
                    v.wait_ge(S_["sV"], C_["sV"])
                    inc(v.tensor_add(wsc[:, 0:4], wsc[:, 0:4], w4[:, 0:4]), "sV")
                    v.wait_ge(S_["sV"], C_["sV"])
                    inc(v.tensor_scalar(out=wsc[:, 4:8], in0=wsc[:, 0:4], scalar1=float(CAP),
                                        scalar2=None, op0=ALU.is_lt), "sV")
                    v.wait_ge(S_["sV"], C_["sV"])
                    inc(v.tensor_mul(gfin[:, 4 * ti:4 * ti + 4], g4[:, 4 * ti:4 * ti + 4],
                                     wsc[:, 4:8]), "sV")
                    inc(v.tensor_scalar_add(w4[:, 0:4], w4[:, 0:4], BIGIDX), "sV")
                    v.wait_ge(S_["sV"], C_["sV"])
                    inc(v.scalar_tensor_tensor(out=w4[:, 0:4], in0=wsc[:, 4:8], scalar=-BIGIDX,
                                               in1=w4[:, 0:4], op0=ALU.mult, op1=ALU.add), "sV")
                    v.wait_ge(S_["sV"], C_["sV"])
                    inc(v.tensor_copy(idx4[:, 4 * ti:4 * ti + 4], w4[:, 0:4]), "sV")
                onV(_)

            def _(g):
                wait(g, "sV", "dG")
                for ti in range(T16):
                    for k in range(4):
                        dinc(g.indirect_dma_start(
                            out=xe_dram, out_offset=IndirectOffsetOnAxis(
                                ap=idx4[:, 4 * ti + k:4 * ti + k + 1], axis=0),
                            in_=p0v[:, ti, :], in_offset=None,
                            bounds_check=bndreg[0], oob_is_err=False), "dG")
            onG(_)

            w1v = WB[:, 0:4096].rearrange("p (a i) -> p a i", i=NI)
            w3v = WB[:, 4096:8192].rearrange("p (a i) -> p a i", i=NI)
            w2v = HB[:, 0:4 * 1024].rearrange("p (b d) -> p b d", d=D)
            for e in range(NE):
                def _(g, e=e):
                    if e == 0:
                        wait(g, "sV", "dG")
                    else:
                        wait(g, "sT")
                    with g.register(f"kg{callid}_{e}") as r:
                        g.reg_load(r, kepti[0:1, e:e + 1])
                        with g.If_cmp(r, 0, "IS_GT"):
                            g.dma_start(out=w1v[:], in_=W1[e].rearrange("(a p) i -> p a i", p=128)
                                        ).then_inc(S_["dG"], 16)
                            g.dma_start(out=w3v[:], in_=W3[e].rearrange("(a p) i -> p a i", p=128)
                                        ).then_inc(S_["dG"], 16)
                            g.dma_start(out=w2v[:], in_=W2[e].rearrange("(b p) d -> p b d", p=128)
                                        ).then_inc(S_["dG"], 16)
                        with g.Else():
                            g.sem_inc(S_["dG"], 48)
                    C_["dG"] += 48
                onG(_)

                for j in range(2):
                    def _(s, e=e, j=j):
                        wait(s, "dG")
                        wait(s, "sT")
                        with s.register(f"ks{callid}_{e}_{j}") as r:
                            s.reg_load(r, kepti[0:1, e:e + 1])
                            with s.If_cmp(r, 512 * j, "IS_GT"):
                                for c in range(DCH):
                                    s.dma_start(out=qcg[:, c, :],
                                                in_=xe_dram[CAP * e + 512 * j:CAP * e + 512 * (j + 1),
                                                            128 * c:128 * (c + 1)],
                                                transpose=True).then_inc(S_["dS"], 16)
                            with s.Else():
                                s.sem_inc(S_["dS"], 16 * DCH)
                        C_["dS"] += 16 * DCH
                    onS(_)

                    for cs in range(4):
                        base = 512 * j + 128 * cs

                        def _(t, e=e, j=j, cs=cs, base=base):
                            wait(t, "dS")
                            if not (e == 0 and j == 0 and cs == 0):
                                wait(t, "sV")
                            with t.register(f"kt{callid}_{e}_{j}_{cs}") as r:
                                t.reg_load(r, kepti[0:1, e:e + 1])
                                with t.If_cmp(r, base, "IS_GT"):
                                    for b in range(4):
                                        for a in range(DCH):
                                            t.matmul(PS[0][:, 128 * b:128 * (b + 1)],
                                                     w1v[:, a, 128 * b:128 * (b + 1)],
                                                     qcg[:, a, 128 * cs:128 * (cs + 1)],
                                                     start=(a == 0), stop=(a == DCH - 1))
                                    for b in range(4):
                                        for a in range(DCH):
                                            mm = t.matmul(PS[1][:, 128 * b:128 * (b + 1)],
                                                          w3v[:, a, 128 * b:128 * (b + 1)],
                                                          qcg[:, a, 128 * cs:128 * (cs + 1)],
                                                          start=(a == 0), stop=(a == DCH - 1))
                                    mm.then_inc(S_["sT"], 1)
                                with t.Else():
                                    t.sem_inc(S_["sT"], 1)
                            C_["sT"] += 1
                        onT(_)

                        def _(s, e=e, base=base):
                            wait(s, "sT")
                            with s.register(f"ksl{callid}_{e}_{base}") as r:
                                s.reg_load(r, kepti[0:1, e:e + 1])
                                with s.If_cmp(r, base, "IS_GT"):
                                    s.activation(out=SC2[:, 0:512], in_=PS[0][:, 0:512],
                                                 func=AF.Silu).then_inc(S_["sS"], 1)
                                with s.Else():
                                    s.sem_inc(S_["sS"], 1)
                            C_["sS"] += 1
                        onS(_)

                        def _(v, e=e, base=base):
                            wait(v, "sS")
                            with v.register(f"kv{callid}_{e}_{base}") as r:
                                v.reg_load(r, kepti[0:1, e:e + 1])
                                with v.If_cmp(r, base, "IS_GT"):
                                    v.tensor_mul(OF[:, 0:512], SC2[:, 0:512],
                                                 PS[1][:, 0:512]).then_inc(S_["sV"], 1)
                                with v.Else():
                                    v.sem_inc(S_["sV"], 1)
                            C_["sV"] += 1
                        onV(_)

                        def _(t, e=e, base=base):
                            wait(t, "sV")
                            with t.register(f"ky{callid}_{e}_{base}") as r:
                                t.reg_load(r, kepti[0:1, e:e + 1])
                                with t.If_cmp(r, base, "IS_GT"):
                                    for dd in range(2):
                                        for b in range(4):
                                            mm = t.matmul(PS[4 + dd][:, 0:512],
                                                          OF[:, 128 * b:128 * (b + 1)],
                                                          w2v[:, b, 512 * dd:512 * (dd + 1)],
                                                          start=(b == 0), stop=(b == 3))
                                    mm.then_inc(S_["sT"], 1)
                                with t.Else():
                                    t.sem_inc(S_["sT"], 1)
                            C_["sT"] += 1
                        onT(_)

                        def _(v, e=e, base=base):
                            wait(v, "sT")
                            with v.register(f"kye{callid}_{e}_{base}") as r:
                                v.reg_load(r, kepti[0:1, e:e + 1])
                                with v.If_cmp(r, base, "IS_GT"):
                                    v.tensor_copy(SC2[:, 1024:1536], PS[4][:, 0:512])
                                    v.tensor_copy(SC2[:, 1536:2048],
                                                  PS[5][:, 0:512]).then_inc(S_["sV"], 1)
                                with v.Else():
                                    v.sem_inc(S_["sV"], 1)
                            C_["sV"] += 1
                        onV(_)

                        def _(y, e=e, base=base):
                            wait(y, "sV")
                            with y.register(f"kst{callid}_{e}_{base}") as r:
                                y.reg_load(r, kepti[0:1, e:e + 1])
                                with y.If_cmp(r, base, "IS_GT"):
                                    y.dma_start(out=ye_dram[CAP * e + base:CAP * e + base + 128, :],
                                                in_=SC2[:, 1024:2048]).then_inc(S_["dY"], 16)
                                with y.Else():
                                    y.sem_inc(S_["dY"], 16)
                            C_["dY"] += 16
                        onY(_)

            for ti in range(T16):
                for half in range(2):
                    def _(g, ti=ti, half=half):
                        if ti == 0 and half == 0:
                            wait(g, "dY")
                        else:
                            wait(g, "sV")
                        for k in (2 * half, 2 * half + 1):
                            dinc(g.indirect_dma_start(
                                out=gbv[:, k - 2 * half, :], out_offset=None,
                                in_=ye_dram, in_offset=IndirectOffsetOnAxis(
                                    ap=idx4[:, 4 * ti + k:4 * ti + k + 1], axis=0),
                                bounds_check=bndreg[0], oob_is_err=False), "dG")
                    onG(_)

                    def _(v, ti=ti, half=half):
                        wait(v, "dG")
                        if half == 0:
                            inc(v.tensor_scalar(out=SC2[:, 0:1024], in0=gbv[:, 0, :],
                                                scalar1=gfin[:, 4 * ti:4 * ti + 1], scalar2=None,
                                                op0=ALU.mult), "sV")
                            v.wait_ge(S_["sV"], C_["sV"])
                            inc(v.scalar_tensor_tensor(out=SC2[:, 0:1024], in0=gbv[:, 1, :],
                                                       scalar=gfin[:, 4 * ti + 1:4 * ti + 2],
                                                       in1=SC2[:, 0:1024], op0=ALU.mult,
                                                       op1=ALU.add), "sV")
                        else:
                            for k in (2, 3):
                                inc(v.scalar_tensor_tensor(out=SC2[:, 0:1024], in0=gbv[:, k - 2, :],
                                                           scalar=gfin[:, 4 * ti + k:4 * ti + k + 1],
                                                           in1=SC2[:, 0:1024], op0=ALU.mult,
                                                           op1=ALU.add), "sV")
                                v.wait_ge(S_["sV"], C_["sV"])
                            inc(v.tensor_add(rv[:, ti, :], rv[:, ti, :], SC2[:, 0:1024]), "sV")
                    onV(_)

            sw1_v = WB[:, 0:4096].rearrange("p (a i) -> p a i", i=NI)
            sw3_v = WB[:, 4096:8192].rearrange("p (a i) -> p a i", i=NI)
            sw2_v = HB[:, 0:4 * 1024].rearrange("p (b d) -> p b d", d=D)

            def _(g):
                wait(g, "sT")
                dinc(g.dma_start(out=sw1_v[:], in_=sw1T.rearrange("(a p) i -> p a i", p=128)), "dG")
                dinc(g.dma_start(out=sw3_v[:], in_=sw3T.rearrange("(a p) i -> p a i", p=128)), "dG")
                dinc(g.dma_start(out=sw2_v[:], in_=sw2T.rearrange("(b p) d -> p b d", p=128)), "dG")
            onG(_)

            for tg in range(4):
                for ic in range(4):
                    def _(t, tg=tg, ic=ic):
                        if tg == 0 and ic == 0:
                            wait(t, "dG", "sV")
                        if ic == 0 and tg > 0:
                            wait(t, "sV")
                        if ic >= 2:
                            wait(t, "sV")
                        b0 = PS[ic % 2]
                        b1 = PS[2 + ic % 2]
                        for a in range(DCH):
                            t.matmul(b0[:, 0:512], sw1_v[:, a, :][:, 128 * ic:128 * (ic + 1)],
                                     p1v[:, a, 512 * tg:512 * (tg + 1)],
                                     start=(a == 0), stop=(a == DCH - 1))
                        for a in range(DCH):
                            mm = t.matmul(b1[:, 0:512], sw3_v[:, a, :][:, 128 * ic:128 * (ic + 1)],
                                          p1v[:, a, 512 * tg:512 * (tg + 1)],
                                          start=(a == 0), stop=(a == DCH - 1))
                        inc(mm, "sT")
                    onT(_)

                    def _(s, tg=tg, ic=ic):
                        wait(s, "sT")
                        inc(s.activation(out=SC2[:, 512 * (ic % 2):512 * (ic % 2) + 512],
                                         in_=PS[ic % 2][:, 0:512], func=AF.Silu), "sS")
                    onS(_)

                    def _(v, tg=tg, ic=ic):
                        wait(v, "sS")
                        inc(v.tensor_mul(ofv[:, ic, :], SC2[:, 512 * (ic % 2):512 * (ic % 2) + 512],
                                         PS[2 + ic % 2][:, 0:512]), "sV")
                    onV(_)

                for tj in range(4):
                    ti = 4 * tg + tj

                    def _(t, tg=tg, tj=tj, ti=ti):
                        wait(t, "sV")
                        for dd in range(2):
                            for ic in range(4):
                                mm = t.matmul(PS[4 + dd][:, 0:512],
                                              ofv[:, ic, 128 * tj:128 * (tj + 1)],
                                              sw2_v[:, ic, 512 * dd:512 * (dd + 1)],
                                              start=(ic == 0), stop=(ic == 3))
                        inc(mm, "sT")
                    onT(_)

                    def _(v, ti=ti):
                        wait(v, "sT")
                        inc(v.tensor_add(rv[:, ti, 0:512], rv[:, ti, 0:512], PS[4][:, 0:512]), "sV")
                        inc(v.tensor_add(rv[:, ti, 512:1024], rv[:, ti, 512:1024],
                                         PS[5][:, 0:512]), "sV")
                    onV(_)

        def attn_b():
            ln_phase(with_xn=False)
            p0c = P0[:].rearrange("p (c t) -> p c t", t=NTOK)
            wqs = QC[:, 0:1024].rearrange("p (a e) -> p a e", e=128)
            wks = QC[:, 1024:2048].rearrange("p (a e) -> p a e", e=128)
            wvs = QC[:, 2048:3072].rearrange("p (a e) -> p a e", e=128)
            wo_v = WB[:, 0:8 * 1024].rearrange("p (a e) -> p a e", e=D)

            vaw = HB[:, 4096:4096 + 2 * 16 * 65].rearrange("p (h t a) -> p h t a", t=T16, a=65)

            def _(g):
                wait(g, "sT")
                dinc(g.dma_start(out=wo_v[:], in_=bwoT.rearrange("(a p) e -> p a e", p=128)), "dG")
            onG(_)

            for hp in range(8):
                def _(g, hp=hp):
                    if hp == 0:
                        wait(g, "dS", "sS")
                    wait(g, "sT", "sV")
                    dinc(g.dma_start(out=wqs[:], in_=bwqT[:, 128 * hp:128 * (hp + 1)]
                                     .rearrange("(a p) e -> p a e", p=128)), "dG")
                    dinc(g.dma_start(out=wks[:], in_=bwkT[:, 128 * hp:128 * (hp + 1)]
                                     .rearrange("(a p) e -> p a e", p=128)), "dG")
                    dinc(g.dma_start(out=wvs[:], in_=bwvT[:, 128 * hp:128 * (hp + 1)]
                                     .rearrange("(a p) e -> p a e", p=128)), "dG")
                    inc(g.memset(vaw[:, :, :, 64:65], 1.0), "sG")
                onG(_)

                for which, dstslot, wv_ in (("q", 0, wqs), ("k", 1, wks)):
                    for tg in range(4):
                        def _(t, which=which, tg=tg, wv_=wv_):
                            if which == "q" and tg == 0:
                                wait(t, "dG", "sG")
                            wait(t, "sV")
                            bank = PS[tg % 2]
                            for a in range(DCH):
                                mm = t.matmul(bank[:, 0:512], wv_[:, a, :],
                                              p1v[:, a, 512 * tg:512 * (tg + 1)],
                                              start=(a == 0), stop=(a == DCH - 1))
                            inc(mm, "sT")
                        onT(_)

                        def _(v, dstslot=dstslot, tg=tg):
                            wait(v, "sT")
                            inc(v.tensor_copy(hbv[:, dstslot, 512 * tg:512 * (tg + 1)],
                                              PS[tg % 2][:, 0:512]), "sV")
                        onV(_)

                for ti in range(T16):
                    def _(t, ti=ti):
                        if ti >= 2:
                            wait(t, "sV")
                        bank = PS[2 + ti % 2]
                        for a in range(DCH):
                            mm = t.matmul(bank[:, 0:128], p1v[:, a, 128 * ti:128 * (ti + 1)],
                                          wvs[:, a, :], start=(a == 0), stop=(a == DCH - 1))
                        inc(mm, "sT")
                    onT(_)

                    def _(v, ti=ti):
                        wait(v, "sT")
                        inc(v.tensor_copy(vaw[:, 0, ti, 0:64], PS[2 + ti % 2][:, 0:64]), "sV")
                        inc(v.tensor_copy(vaw[:, 1, ti, 0:64], PS[2 + ti % 2][:, 64:128]), "sV")
                    onV(_)

                for which, slot, col in (("q", 0, 4), ("k", 1, 5)):
                    def _(s, slot=slot):
                        wait(s, "sV")
                        if which == "k":
                            wait(s, "sV")
                        inc(s.activation(out=SC2[:, 0:2048], in_=hbv[:, slot, :],
                                         func=AF.Square), "sS")
                    onS(_)

                    def _(v, col=col):
                        inc(v.memset(scal[0:1, col:col + 1], 0.0), "sV")
                    onV(_)

                    for tg in range(4):
                        def _(t, tg=tg):
                            wait(t, "sS", "sV")
                            inc(t.matmul(PS[6][0:1, 0:512], c_onescol[:],
                                         SC2[:, 512 * tg:512 * (tg + 1)],
                                         start=True, stop=True), "sT")
                        onT(_)

                        def _(v, col=col):
                            wait(v, "sT")
                            inc(v.tensor_reduce(out=scal[0:1, 6:7], in_=PS[6][0:1, 0:512],
                                                axis=AX.X, op=ALU.max), "sV")
                            v.wait_ge(S_["sV"], C_["sV"])
                            inc(v.tensor_tensor(out=scal[0:1, col:col + 1],
                                                in0=scal[0:1, col:col + 1],
                                                in1=scal[0:1, 6:7], op=ALU.max), "sV")
                        onV(_)

                def _(v):
                    wait(v, "sV")
                    inc(v.tensor_mul(scal[0:1, 6:7], scal[0:1, 4:5], scal[0:1, 5:6]), "sV")
                onV(_)

                def _(s):
                    wait(s, "sV")
                    inc(s.activation(out=scal[0:1, 7:8], in_=scal[0:1, 6:7], func=AF.Sqrt,
                                     scale=1.0 / 64.0), "sS")
                    s.wait_ge(S_["sS"], C_["sS"])
                    inc(s.mul(out=scal[0:1, 6:7], in_=scal[0:1, 7:8], mul=-1.0), "sS")
                onS(_)

                def _(t):
                    wait(t, "sS")
                    inc(t.matmul(PS[6][:, 0:1], c_onesrow[:], scal[0:1, 6:7],
                                 start=True, stop=True), "sT")
                onT(_)

                def _(v):
                    wait(v, "sT")
                    inc(v.tensor_copy(negb[:, 0:1], PS[6][:, 0:1]), "sV")
                onV(_)

                SEM = S_
                qT, qS, qV, qG = [], [], [], []
                v_ready = C_["sV"]
                for h2 in range(2):
                    for g in range(4):
                        nkj = 4 * g + 4
                        npr = nkj // 2
                        v_mm, v_exp, v_aff, v_pv = {}, {}, {}, {}
                        v_prevdiv = C_["sV"]

                        def emit_pvs(pr2, h2=h2, nkj=nkj):
                            for kj2 in (2 * pr2, 2 * pr2 + 1):
                                slot = kj2 % 4
                                wv = [("sS", v_exp[pr2])]
                                if kj2 in v_aff:
                                    wv.append(("sG", v_aff[kj2]))
                                C_["sT"] += 1
                                v_pv[kj2] = C_["sT"]
                                def f(t, kj2=kj2, h2=h2, slot=slot, nkj=nkj, wv=tuple(wv)):
                                    for sem, val in wv:
                                        t.wait_ge(SEM[sem], val)
                                    t.matmul(PS[4 + h2][0:65, 0:512], vaw[:, h2, kj2, :],
                                             ofv[:, slot, :], start=(kj2 == 0),
                                             stop=(kj2 == nkj - 1)).then_inc(SEM["sT"], 1)
                                qT.append(f)

                        for pr in range(npr):
                            dbl = pr % 2
                            for half in range(2):
                                kj = 2 * pr + half
                                wv = []
                                if kj == 0:
                                    wv.append(("sV", max(v_ready, v_prevdiv)))
                                if pr >= 2 and half == 0:
                                    wv.append(("sS", v_exp[pr - 2]))
                                C_["sT"] += 1
                                v_mm[kj] = C_["sT"]
                                def f(t, kj=kj, h2=h2, g=g, half=half, dbl=dbl, wv=tuple(wv)):
                                    for sem, val in wv:
                                        t.wait_ge(SEM[sem], val)
                                    t.matmul(PSDs[dbl][:, 512 * half:512 * half + 512],
                                             hbv[:, 1, :][64 * h2:64 * h2 + 64,
                                                          128 * kj:128 * (kj + 1)],
                                             hbv[:, 0, :][64 * h2:64 * h2 + 64,
                                                          512 * g:512 * (g + 1)],
                                             start=True, stop=True).then_inc(SEM["sT"], 1)
                                qT.append(f)

                            pb = pr % 2
                            wv = [("sT", v_mm[2 * pr + 1])]
                            if pr >= 2:
                                wv.append(("sT", v_pv[2 * (pr - 2) + 1]))
                                a0 = v_aff.get(2 * (pr - 2))
                                a1 = v_aff.get(2 * (pr - 2) + 1)
                                av = max([x for x in (a0, a1) if x is not None],
                                         default=None)
                                if av is not None:
                                    wv.append(("sG", av))
                            C_["sS"] += 1
                            v_exp[pr] = C_["sS"]
                            def f(sc, pb=pb, dbl=dbl, wv=tuple(wv)):
                                for sem, val in wv:
                                    sc.wait_ge(SEM[sem], val)
                                sc.activation(out=OF[:, 1024 * pb:1024 * pb + 1024],
                                              in_=PSDs[dbl][:, 0:1024], func=AF.Exp,
                                              bias=negb[:, 0:1], scale=0.125
                                              ).then_inc(SEM["sS"], 1)
                            qS.append(f)

                            for kj2 in (2 * pr, 2 * pr + 1):
                                if kj2 >= 4 * g:
                                    o = kj2 - 4 * g
                                    slot = kj2 % 4
                                    C_["sG"] += 1
                                    v_aff[kj2] = C_["sG"]
                                    def f(gp, slot=slot, o=o, ve=v_exp[pr]):
                                        gp.wait_ge(SEM["sS"], ve)
                                        gp.affine_select(
                                            out=ofv[:, slot, :], in_=ofv[:, slot, :],
                                            pattern=[[1, 512]], compare_op=ALU.is_ge,
                                            fill=0.0, base=-128 * o,
                                            channel_multiplier=-1).then_inc(SEM["sG"], 1)
                                    qG.append(f)

                            if pr >= 1:
                                emit_pvs(pr - 1)
                        emit_pvs(npr - 1)

                        v_prevchain = C_["sV"]
                        C_["sV"] += 1
                        va = C_["sV"]
                        C_["sV"] += 1
                        vb = C_["sV"]
                        def f(v, h2=h2, wpv=v_pv[nkj - 1], vp=v_prevchain, va=va):
                            v.wait_ge(SEM["sT"], wpv)
                            v.wait_ge(SEM["sV"], vp)
                            v.tensor_copy(srow_s, PS[4 + h2][64:65, 0:512]).then_inc(SEM["sV"], 1)
                            v.wait_ge(SEM["sV"], va)
                            v.reciprocal(srow_r, srow_s).then_inc(SEM["sV"], 1)
                        qV.append(f)
                        C_["sT"] += 1
                        vm = C_["sT"]
                        def f(t, vb=vb):
                            t.wait_ge(SEM["sV"], vb)
                            t.matmul(PS[6][0:64, 0:512], c_onesrow[0:1, 0:64], srow_r,
                                     start=True, stop=True).then_inc(SEM["sT"], 1)
                        qT.append(f)
                        C_["sV"] += 1
                        vc = C_["sV"]
                        C_["sV"] += 1
                        def f(v, h2=h2, g=g, hp=hp, vm=vm, vc=vc):
                            v.wait_ge(SEM["sT"], vm)
                            v.tensor_copy(rbc, PS[6][0:64, 0:512]).then_inc(SEM["sV"], 1)
                            v.wait_ge(SEM["sV"], vc)
                            dst = p0c[:, hp, 512 * g:512 * (g + 1)][64 * h2:64 * h2 + 64, :]
                            v.tensor_mul(dst, PS[4 + h2][0:64, 0:512], rbc).then_inc(SEM["sV"], 1)
                        qV.append(f)

                def mk(fns):
                    def body(eng):
                        for fn in fns:
                            fn(eng)
                    return body
                if qT:
                    onT(mk(qT))
                if qS:
                    onS(mk(qS))
                if qG:
                    onG(mk(qG))
                if qV:
                    onV(mk(qV))

            for ti in range(T16):
                for dd in range(2):
                    def _(t, ti=ti, dd=dd):
                        if ti == 0 and dd == 0:
                            wait(t, "sV", "dG")
                        if ti >= 1 and dd == 0:
                            wait(t, "sV")
                        for c in range(DCH):
                            mm = t.matmul(PS[dd][:, 0:512],
                                          p0c[:, c, 128 * ti:128 * (ti + 1)],
                                          wo_v[:, c, 512 * dd:512 * (dd + 1)],
                                          start=(c == 0), stop=(c == DCH - 1))
                        inc(mm, "sT")
                    onT(_)

                def _(v, ti=ti):
                    wait(v, "sT")
                    inc(v.tensor_add(rv[:, ti, 0:512], rv[:, ti, 0:512], PS[0][:, 0:512]), "sV")
                    inc(v.tensor_add(rv[:, ti, 512:1024], rv[:, ti, 512:1024], PS[1][:, 0:512]), "sV")
                onV(_)

        attn_a()
        if stage in ("f1", "b", "full"):
            moe_ffn(0)
        if stage in ("b", "full"):
            attn_b()
        if stage == "full":
            moe_ffn(1)

        if dbg:
            dP0 = nc.dram_tensor("dP0", [128, T16 * D], F32, kind="ExternalOutput").ap()
            dP1 = nc.dram_tensor("dP1", [128, T16 * D], F32, kind="ExternalOutput").ap()
            dHB = nc.dram_tensor("dHB", [128, 6272], F32, kind="ExternalOutput").ap()
            dSC = nc.dram_tensor("dSC", [128, 2048], F32, kind="ExternalOutput").ap()
            dsc = nc.dram_tensor("dsc", [128, 8], F32, kind="ExternalOutput").ap()
            dnb = nc.dram_tensor("dnb", [128, 2], F32, kind="ExternalOutput").ap()
            dsp0 = nc.dram_tensor("dsp0", [NTOK, D], F32, kind="ExternalOutput").ap()
            dsp1 = nc.dram_tensor("dsp1", [NTOK, D], F32, kind="ExternalOutput").ap()
            didx = nc.dram_tensor("didx", [128, T16 * 4], I32, kind="ExternalOutput").ap()
            dgf = nc.dram_tensor("dgf", [128, T16 * 4], F32, kind="ExternalOutput").ap()
            dg4 = nc.dram_tensor("dg4", [128, T16 * 4], F32, kind="ExternalOutput").ap()
            dmif = nc.dram_tensor("dmif", [128, T16 * 4], F32, kind="ExternalOutput").ap()
            dkept = nc.dram_tensor("dkept", [1, NE], F32, kind="ExternalOutput").ap()
            dcnt = nc.dram_tensor("dcnt", [1, NE], F32, kind="ExternalOutput").ap()
            doff = nc.dram_tensor("doff", [1, (T16 + 1) * NE], F32, kind="ExternalOutput").ap()
            dsg = nc.dram_tensor("dsg", [128, T16 * NE], F32, kind="ExternalOutput").ap()

            def _(g):
                wait(g, "sV", "sS", "sT", "sG", "dS", "dG", "dY")
                dinc(g.dma_start(out=dP0, in_=P0[:]), "dG")
                dinc(g.dma_start(out=dP1, in_=P1[:]), "dG")
                dinc(g.dma_start(out=dHB, in_=HB[:]), "dG")
                dinc(g.dma_start(out=dSC, in_=SC2[:]), "dG")
                dinc(g.dma_start(out=dsc, in_=scal[:]), "dG")
                dinc(g.dma_start(out=dnb, in_=negb[:]), "dG")
                dinc(g.dma_start(out=dsp0, in_=sp0), "dG")
                dinc(g.dma_start(out=dsp1, in_=sp1), "dG")
                dinc(g.dma_start(out=didx, in_=idx4[:]), "dG")
                dinc(g.dma_start(out=dgf, in_=gfin[:]), "dG")
                dinc(g.dma_start(out=dg4, in_=g4[:]), "dG")
                dinc(g.dma_start(out=dmif, in_=mif[:]), "dG")
                dinc(g.dma_start(out=dkept, in_=keptf[:]), "dG")
                dinc(g.dma_start(out=dcnt, in_=cntrow[:]), "dG")
                dinc(g.dma_start(out=doff, in_=offrow[:]), "dG")
                dinc(g.dma_start(out=dsg, in_=sgate[:]), "dG")
            onG(_)

        def _(y):
            wait(y, "sV", "sS", "sT", "sG", "dS", "dG")
            dinc(y.dma_start(out=yout.rearrange("(t p) d -> p t d", p=128), in_=rv[:]), "dY")
            y.wait_ge(S_["dY"], C_["dY"])
        onY(_)

    p.ctx.close()
    return p


def _stage_inputs(inputs, core):
    f32 = np.float32
    bf = ml_dtypes.bfloat16
    x = np.ascontiguousarray(np.asarray(inputs["L0_attn_a_x"])[core], f32)

    def t(a):
        return np.ascontiguousarray(np.asarray(a))

    def tt(a):
        return np.ascontiguousarray(np.asarray(a).T)

    qs = np.asarray(inputs["L0_attn_a_q_scale"], f32)
    ks = np.asarray(inputs["L0_attn_a_k_scale"], f32)
    nkv = np.asarray(inputs["L0_attn_a_null_kv"], f32)
    d = {
        "xin": x,
        "aWq": t(inputs["L0_attn_a_Wq"]),
        "aWkv": t(inputs["L0_attn_a_Wkv"]),
        "aWout": t(inputs["L0_attn_a_Wout"]),
        "bwqT": tt(inputs["L0_attn_b_wq"]),
        "bwkT": tt(inputs["L0_attn_b_wk"]),
        "bwvT": tt(inputs["L0_attn_b_wv"]),
        "bwoT": tt(inputs["L0_attn_b_wo"]),
        "gateT": tt(inputs["L0_ffn_gate_w"]),
        "W1": t(inputs["L0_ffn_W1"]),
        "W3": t(inputs["L0_ffn_W3"]),
        "W2": t(inputs["L0_ffn_W2"]),
        "sw1T": tt(inputs["L0_ffn_sw1"]),
        "sw3T": tt(inputs["L0_ffn_sw3"]),
        "sw2T": tt(inputs["L0_ffn_sw2"]),
        "gamma_bc": np.ascontiguousarray(np.broadcast_to(
            np.asarray(inputs["L0_attn_a_gamma"], f32)[None, :], (128, D))),
        "qscale_bc": np.ascontiguousarray(np.broadcast_to(
            np.tile(qs, NH)[None, :], (128, D))),
        "kscale_bc": np.ascontiguousarray(np.broadcast_to(ks[None, :], (128, DH))),
        "qsrow": np.ascontiguousarray(qs[None, :]),
        "ksrow": np.ascontiguousarray(ks[None, :]),
        "nullk": np.ascontiguousarray(nkv[0]),
        "nullv": np.ascontiguousarray(nkv[1]),
        "biasg_bc": np.ascontiguousarray(np.broadcast_to(
            np.asarray(inputs["L0_ffn_bias"], f32)[None, :], (128, NE))),
        "prefmask": (np.arange(8) < core).astype(f32).reshape(8, 1),
        "identbf": np.eye(128, dtype=bf),
        "triUf": np.triu(np.ones((128, 128), f32), 1),
        "onesrow": np.ones((1, 128), f32),
        "onescol": np.ones((128, 1), f32),
        "eidx_bc": np.ascontiguousarray(np.broadcast_to(
            np.arange(NE, dtype=f32)[None, :], (128, NE))),
    }
    return d


_prog_cache = {}


_DBG = False
_last_results = None


def kernel(**inputs):
    global _last_results
    stage = _STAGE
    key = (stage, _DBG)
    if key not in _prog_cache:
        _prog_cache[key] = build_program(stage, dbg=_DBG)
    p = _prog_cache[key]
    in_maps = [_stage_inputs(inputs, c) for c in range(8)]
    resl = run_bass_kernel_spmd(p.nc, in_maps, core_ids=list(range(8)))
    _last_results = resl.results
    out = np.concatenate([r["yout"][None] for r in resl.results], axis=0)
    return out.astype(np.float32)
